# version 10
# baseline (speedup 1.0000x reference)
import numpy as np
import ml_dtypes
import concourse.bass as bass
import concourse.mybir as mybir
from concourse.bass_utils import run_bass_kernel_spmd

F32 = mybir.dt.float32
BF16 = mybir.dt.bfloat16
AF = mybir.ActivationFunctionType
ALU = mybir.AluOpType
AX = mybir.AxisListType

J = 2048
T = 1024
S = 2048
NJ = 16
NT1 = 8
NS1 = 16
L = 1024
NL = 8
HE = [1024, 2048, 3072, 4096, 1024, 2048, 3072, 4096]
NCH = [h // 128 for h in HE]
CH_BASE = np.cumsum([0] + NCH).tolist()
NCH_TOT = sum(NCH)
SH_B1 = NCH_TOT
EPS_ATTN = 1e-6
EPS_LN = 1e-5


class Ctr:
    def __init__(self, handle):
        self.h = handle
        self.v = 0

    def add(self, n=1):
        self.v += n
        return self.v


def _groups():
    gs = []
    for e in range(8):
        hc0 = 0
        while hc0 < NCH[e]:
            n = min(8, NCH[e] - hc0)
            gs.append(dict(e=e, shared=False, hc0=hc0, nch=n, gb=CH_BASE[e]))
            hc0 += n
    gs.append(dict(e=None, shared=True, hc0=0, nch=8, gb=SH_B1))
    return gs


GROUPS = _groups()
N_PHANT = sum(g["nch"] for g in GROUPS if not g["shared"])
DEBUG_DUMPS = False
PHASE_LIMIT = 99
K_MODE = "full"


class _StopBuild(Exception):
    pass


def build_nc():
    nc = bass.Bass("TRN2", target_bir_lowering=False,
                   detect_race_conditions=False)

    def din(name, shape, dt=F32):
        return nc.dram_tensor(name, list(shape), dt, kind="ExternalInput")

    xt = din("xt", (J, S))
    wqt = din("wqt", (J, J))
    wkt = din("wkt", (J, 512))
    wvt = din("wvt", (J, 512))
    wot = din("wot", (J, J))
    bq_t = din("bq_t", (128, NJ))
    bk_row = din("bk_row", (1, 512))
    bv_row = din("bv_row", (1, 512))
    down_w = din("down_w", (J, L))
    down_b_t = din("down_b_t", (128, NL))
    up_w = din("up_w", (L, J))
    up_b_t = din("up_b_t", (128, NJ))
    router_w = din("router_w", (J, 8))
    rb_col = din("rb_col", (8, 1))
    ncost_col = din("ncost_col", (8, 1))
    shared_w1 = din("shared_w1", (L, L))
    shared_b1_t = din("shared_b1_t", (128, 8))
    shared_w2 = din("shared_w2", (L, L))
    shared_b2_row = din("shared_b2_row", (1, L))
    core_w = din("core_w", (J, J))
    core_b_t = din("core_b_t", (128, NJ))
    w1d = [din(f"w1_{e}", (L, HE[e])) for e in range(8)]
    w2d = [din(f"w2_{e}", (HE[e], L)) for e in range(8)]
    b1d = [din(f"b1_{e}_t", (128, NCH[e])) for e in range(8)]
    b2all = din("b2all", (8, L))
    ident_bf = din("ident_bf", (128, 128), BF16)
    ident8_f = din("ident8_f", (8, 8))
    ones_row_bf = din("ones_row_bf", (1, 128), BF16)
    ones_col_bf = din("ones_col_bf", (128, 1), BF16)
    sgn17_col = din("sgn17_col", (17, 1), BF16)
    p01_col = din("p01_col", (1, 1), BF16)

    out = nc.dram_tensor("out", [J, T], F32, kind="ExternalOutput")

    import contextlib
    stx = contextlib.ExitStack()

    def sb(name, shape, dt):
        return stx.enter_context(nc.sbuf_tensor(name, list(shape), dt))

    pb = [stx.enter_context(nc.psum_tensor(f"pb{i}", [128, 512], F32))
          for i in range(6)]
    pbx = [stx.enter_context(nc.psum_tensor(f"pbx{i}", [128, 512], F32))
           for i in range(2)]
    pbf = [t.bitcast(BF16)[:, 0:128] for t in pbx]

    arA = sb("arA", (128, 16384), F32)
    arB = sb("arB", (128, 8192), F32)
    arC = sb("arC", (128, 8192), F32)
    arE = sb("arE", (128, 4096), F32)
    arF = sb("arF", (128, 4224), F32)
    wsl = [sb(f"wsl{i}", (128, 2048), F32) for i in range(2)]
    stg = sb("stg", (128, 1024), F32)
    w2st = [sb(f"w2st{i}", (128, 1024), F32) for i in range(2)]
    tscr = sb("tscr", (128, 3072), F32)

    def bfv(t):
        return t[:].bitcast(BF16)

    xtb = lambda jc: bfv(arA)[:, jc * 2048:(jc + 1) * 2048]
    y1v = lambda jc: arA[:, jc * 1024:(jc + 1) * 1024]
    sqv = lambda h: bfv(arB)[:, h * 1024:(h + 1) * 1024]
    x2bv = lambda jc: bfv(arB)[:, jc * 1024:(jc + 1) * 1024]
    w2g = lambda i: bfv(arB)[:, i * 1024:(i + 1) * 1024]
    atv = lambda fc: bfv(arC)[:, fc * 1024:(fc + 1) * 1024]
    accv = lambda ti: arC[:, ti * 1024:(ti + 1) * 1024]
    gx2v = lambda jc: bfv(arC)[:, jc * 1024:(jc + 1) * 1024]
    skv = lambda s_: bfv(arE)[:, s_ * 512:(s_ + 1) * 512]
    gTv = lambda lc: bfv(arE)[:, lc * 1024:(lc + 1) * 1024]
    vaug = lambda s_: bfv(arF)[:, s_ * 520:s_ * 520 + 520]
    Uv = lambda i: bfv(arF)[:, i * 1024:(i + 1) * 1024]
    moev = lambda lc: bfv(arF)[:, lc * 1024:(lc + 1) * 1024]
    logit = arF[0:8, 0:1024]
    wslv = lambda i: bfv(wsl[i])
    upslv = lambda i: bfv(tscr)[:, i * 2048:(i + 1) * 2048]
    stgv = lambda i: stg[:, (i % 4) * 256:(i % 4) * 256 + 256]
    t1 = [tscr[:, 0:512], tscr[:, 512:1024]]
    t2 = [tscr[:, 1024:1536], tscr[:, 1536:2048]]
    t3 = [tscr[:, 2048:2560], tscr[:, 2560:3072]]
    r_m = stg[0:1, 0:512]
    r_t = stg[0:1, 512:1024]
    r_r = w2st[0][0:1, 0:512]
    r_mr = w2st[0][0:1, 512:1024]
    r_rb = w2st[1][0:1, 0:512].bitcast(BF16)[:, 0:512]
    r_mrb = w2st[1][0:1, 512:1024].bitcast(BF16)[:, 0:512]
    bkr = w2st[0][0:1, 0:512]
    bkrow = sb("bkrow", (1, 512), BF16)
    bkrb = bkrow[0:1, 0:512]
    b2f = w2st[0][0:8, :]
    b2sf = w2st[1][0:1, :]
    rows17 = stg[0:17, :].bitcast(BF16)[:, 0:1024]
    b2bf = stg[0:8, :].bitcast(BF16)[:, 1024:2048]
    waug = tscr[0:9, 2048:2560].bitcast(BF16)[:, 0:1024]

    tbx = [sb(f"tb_{i}", (128, 512), BF16) for i in range(2)]
    bqs = sb("bqs", (128, NJ), F32)
    dbs = sb("dbs", (128, NL), F32)
    ubs = sb("ubs", (128, NJ), F32)
    cbs = sb("cbs", (128, NJ), F32)
    bsum = sb("bsum", (128, NJ), F32)
    b1all = sb("b1all", (128, NCH_TOT + 8), F32)
    gb1all = sb("gb1all", (128, NCH_TOT), BF16)
    phg = sb("phg", (128, 64), BF16)
    rbc = sb("rbc", (8, 1), F32)
    ncc = sb("ncc", (8, 1), F32)
    idb = sb("idb", (128, 128), BF16)
    id8 = sb("id8", (8, 8), F32)
    onr = sb("onr", (1, 128), BF16)
    onc = sb("onc", (128, 1), BF16)
    sgn = sb("sgn", (17, 1), BF16)
    p01 = sb("p01", (1, 1), BF16)
    rws = sb("rws", (128, NJ * 8), F32)
    kvaug = sb("kvaug", (128, 4 * 130), BF16)
    wgt = sb("wgt", (128, NT1 * 8), F32)
    wbt = sb("wbt", (128, NT1 * 8), BF16)

    rt = [sb(f"rt{i}", (128, 8), F32) for i in range(4)]
    rc = [sb(f"rc{i}", (128, 1), F32) for i in range(6)]

    sems = {}
    for name in ["ldb", "stf", "gpc",
                 "aca", "acb", "dva", "dvb", "mma", "mmb", "mmc"]:
        sems[name] = Ctr(stx.enter_context(nc.semaphore(name)))
    ldb = sems["ldb"]
    stf, gpc = sems["stf"], sems["gpc"]
    aca, acb, dva, dvb = (sems[k] for k in ["aca", "acb", "dva", "dvb"])
    mma, mmb, mmc = sems["mma"], sems["mmb"], sems["mmc"]
    ldp = [Ctr(stx.enter_context(nc.semaphore(f"ldp{i}"))) for i in range(4)]
    dvs = Ctr(stx.enter_context(nc.semaphore("dvs")))
    acs = Ctr(stx.enter_context(nc.semaphore("acs")))
    gps = Ctr(stx.enter_context(nc.semaphore("gps")))

    def chain(eng, instr, ctr):
        instr.then_inc(ctr.h, 1)
        ctr.add(1)
        eng.wait_ge(ctr.h, ctr.v)
    ldq = [Ctr(stx.enter_context(nc.semaphore(f"ldq{i}"))) for i in range(2)]
    ldx2 = [Ctr(stx.enter_context(nc.semaphore(f"ldx{i}"))) for i in range(2)]

    try:
        with nc.Block("init") as blk:
            @blk.sync
            def _(sy):
                pairs = [(bq_t, bqs), (down_b_t, dbs), (up_b_t, ubs),
                         (core_b_t, cbs), (rb_col, rbc), (ncost_col, ncc),
                         (ident_bf, idb), (ident8_f, id8), (ones_row_bf, onr),
                         (ones_col_bf, onc), (sgn17_col, sgn), (p01_col, p01)]
                for t_, s_ in pairs:
                    sy.dma_start(s_[0:t_.shape[0], :], t_.ap()).then_inc(ldb.h, 16)
                    ldb.add(16)
                sy.dma_start(b1all[:, SH_B1:SH_B1 + 8],
                             shared_b1_t.ap()).then_inc(ldb.h, 16)
                ldb.add(16)
                for e in range(8):
                    sy.dma_start(b1all[:, CH_BASE[e]:CH_BASE[e] + NCH[e]],
                                 b1d[e].ap()).then_inc(ldb.h, 16)
                    ldb.add(16)
                for jc in range(NJ):
                    sy.dma_start(rws[:, jc * 8:(jc + 1) * 8],
                                 router_w.ap()[jc * 128:(jc + 1) * 128, :]
                                 ).then_inc(ldb.h, 16)
                    ldb.add(16)

            nb_all = ldb.v

            @blk.gpsimd
            def _(gp):
                dvs_b0 = dvs.v
                for jc in range(NJ):
                    if jc >= 1:
                        gp.wait_ge(dvs.h, dvs_b0 + 3 + 2 * jc)
                    gp.dma_start(w2st[0][:, :],
                                 xt.ap()[jc * 128:(jc + 1) * 128, 0:1024]
                                 ).then_inc(ldx2[0].h, 16)
                    ldx2[0].add(16)
                    gp.dma_start(w2st[1][:, :],
                                 xt.ap()[jc * 128:(jc + 1) * 128, 1024:2048]
                                 ).then_inc(ldx2[1].h, 16)
                    ldx2[1].add(16)

            @blk.scalar
            def _(sc):
                sc.wait_ge(ldb.h, nb_all)
                for e in range(8):
                    sc.activation(gb1all[:, CH_BASE[e]:CH_BASE[e] + NCH[e]],
                                  b1all[:, CH_BASE[e]:CH_BASE[e] + NCH[e]],
                                  AF.Gelu).then_inc(aca.h, 1)
                    aca.add(1)

            gb1_done = aca.v

            @blk.vector
            def _(v):
                v.memset(kvaug[:], 0.0)
                chain(v, v.memset(rc[0][:, 0:1], 1.0), dvs)
                chain(v, v.reciprocal(rc[1][:, 0:1], rc[0][:, 0:1]), dvs)
                chain(v, v.reciprocal(rc[1][:, 0:1], rc[0][:, 0:1]), dvs)
                for jc in range(NJ):
                    v.wait_ge(ldx2[0].h, 16 * (jc + 1))
                    v.tensor_copy(xtb(jc)[:, 0:1024], w2st[0][:, :]
                                  ).then_inc(dvs.h, 1)
                    dvs.add(1)
                    v.wait_ge(ldx2[1].h, 16 * (jc + 1))
                    v.tensor_copy(xtb(jc)[:, 1024:2048], w2st[1][:, :]
                                  ).then_inc(dvs.h, 1)
                    dvs.add(1)
                v.wait_ge(dvs.h, dvs.v)
                v.nop().then_inc(dva.h, 1)
                dva.add(1)
            dvs_init_done = dvs.v

        dbg_ctr = Ctr(stx.enter_context(nc.semaphore("dbg"))) if DEBUG_DUMPS \
            else None

        def _ck(n):
            if PHASE_LIMIT <= n:
                raise _StopBuild()

        def dump(tag, ap, shape, dt):
            if not DEBUG_DUMPS:
                return
            d = nc.dram_tensor(f"dbg_{tag}", list(shape), dt, kind="ExternalOutput")
            with nc.Block(f"dump_{tag}") as bk2:
                @bk2.gpsimd
                def _(gp):
                    gp.dma_start(d.ap(), ap).then_inc(dbg_ctr.h, 16)
                    dbg_ctr.add(16)
                    gp.wait_ge(dbg_ctr.h, dbg_ctr.v)

        _ck(1)
        with nc.Block("kpass") as blk:
            ldb_b, aca_b, acb_b, mma_b, dva_b = ldb.v, aca.v, acb.v, mma.v, dva.v
            ldp_b = [s.v for s in ldp]

            @blk.sync
            def _(sy):
                sy.dma_start(bkr, bk_row.ap()).then_inc(ldb.h, 16)
                ldb.add(16)
                for jc in range(NJ):
                    if jc >= 2:
                        sy.wait_ge(aca.h, aca_b + jc)
                    sem = ldp[jc % 2]
                    sy.dma_start(stg[:, (jc % 2) * 512:(jc % 2) * 512 + 512],
                                 wkt.ap()[jc * 128:(jc + 1) * 128, :]
                                 ).then_inc(sem.h, 16)
                    sem.add(16)

            @blk.scalar
            def _(sc):
                sc.wait_ge(ldb.h, ldb_b + 16)
                sc.activation(bkrb, bkr, AF.Copy).then_inc(aca.h, 1)
                aca.add(1)
                for jc in range(NJ):
                    sc.wait_ge(ldp[jc % 2].h, ldp_b[jc % 2] + 16 * (jc // 2 + 1))
                    sc.activation(wslv(jc // 8)[:, (jc % 8) * 512:(jc % 8) * 512 + 512],
                                  stg[:, (jc % 2) * 512:(jc % 2) * 512 + 512],
                                  AF.Copy).then_inc(aca.h, 1)
                    aca.add(1)
                if K_MODE == "full":
                    for s_ in range(NS1):
                        sc.wait_ge(mma.h, mma_b + min(s_ + 2, NS1 + 1))
                        sc.activation(t1[s_ % 2], pb[s_ % 4][:], AF.Relu
                                      ).then_inc(acb.h, 1)
                        acb.add(1)
                        sc.wait_ge(dva.h, dva_b + 2 * s_ + 1)
                        sc.activation(t3[s_ % 2], t2[s_ % 2], AF.Exp
                                      ).then_inc(acb.h, 1)
                        acb.add(1)

            @blk.tensor
            def _(te):
                if K_MODE != "nomm":
                    te.wait_ge(aca.h, aca_b + NJ + 1)
                    for s_ in range(NS1):
                        if s_ >= 4:
                            te.wait_ge(dva.h, dva_b + 2 * (s_ - 4) + 2)
                            if K_MODE == "full":
                                te.wait_ge(acb.h, acb_b + 2 * (s_ - 4) + 1)
                        for jc in range(NJ):
                            te.matmul(pb[s_ % 4][:, 0:512],
                                      xtb(jc)[:, s_ * 128:(s_ + 1) * 128],
                                      wslv(jc // 8)[:, (jc % 8) * 512:(jc % 8) * 512 + 512],
                                      start=(jc == 0), stop=False)
                        te.matmul(pb[s_ % 4][:, 0:512], onr[0:1, 0:128],
                                  bkrb, start=False, stop=True
                                  ).then_inc(mma.h, 1)
                        mma.add(1)
                        if s_ == NS1 - 1:
                            te.drain().then_inc(mma.h, 1)
                            mma.add(1)

            @blk.vector
            def _(v):
                for s_ in range(NS1):
                    if K_MODE != "nomm":
                        v.wait_ge(mma.h, mma_b + min(s_ + 2, NS1 + 1))
                    if K_MODE == "full":
                        v.wait_ge(acb.h, acb_b + 2 * s_ + 1)
                        v.tensor_scalar_min(t2[s_ % 2], pb[s_ % 4][:], 0.0
                                            ).then_inc(dva.h, 1)
                        dva.add(1)
                        v.wait_ge(acb.h, acb_b + 2 * s_ + 2)
                        v.tensor_add(skv(s_), t1[s_ % 2], t3[s_ % 2]
                                     ).then_inc(dva.h, 1)
                        dva.add(1)
                    elif K_MODE == "noepi":
                        v.tensor_copy(skv(s_), pb[s_ % 2][:]).then_inc(dva.h, 2)
                        dva.add(2)
                    else:
                        v.tensor_copy(skv(s_), t1[s_ % 2]).then_inc(dva.h, 2)
                        dva.add(2)

        _ck(2)
        with nc.Block("vpass") as blk:
            ldb_b, aca_b, mma_b, mmb_b, dva_b = ldb.v, aca.v, mma.v, mmb.v, dva.v
            ldp_b = [s.v for s in ldp]

            @blk.sync
            def _(sy):
                sy.dma_start(bkr, bv_row.ap()).then_inc(ldb.h, 16)
                ldb.add(16)
                for jc in range(NJ):
                    if jc >= 2:
                        sy.wait_ge(aca.h, aca_b + jc)
                    sem = ldp[jc % 2]
                    sy.dma_start(stg[:, (jc % 2) * 512:(jc % 2) * 512 + 512],
                                 wvt.ap()[jc * 128:(jc + 1) * 128, :]
                                 ).then_inc(sem.h, 16)
                    sem.add(16)

            @blk.scalar
            def _(sc):
                sc.wait_ge(ldb.h, ldb_b + 16)
                sc.activation(bkrb, bkr, AF.Copy).then_inc(aca.h, 1)
                aca.add(1)
                for jc in range(NJ):
                    sc.wait_ge(ldp[jc % 2].h, ldp_b[jc % 2] + 16 * (jc // 2 + 1))
                    sc.activation(wslv(jc // 8)[:, (jc % 8) * 512:(jc % 8) * 512 + 512],
                                  stg[:, (jc % 2) * 512:(jc % 2) * 512 + 512],
                                  AF.Copy).then_inc(aca.h, 1)
                    aca.add(1)

            @blk.tensor
            def _(te):
                te.wait_ge(aca.h, aca_b + NJ + 1)
                for s_ in range(NS1 + 1):
                    if s_ < NS1:
                        if s_ >= 4:
                            te.wait_ge(dva.h, dva_b + (s_ - 4) + 1)
                        for jc in range(NJ):
                            te.matmul(pb[s_ % 4][:, 0:512],
                                      xtb(jc)[:, s_ * 128:(s_ + 1) * 128],
                                      wslv(jc // 8)[:, (jc % 8) * 512:(jc % 8) * 512 + 512],
                                      start=(jc == 0), stop=False)
                        te.matmul(pb[s_ % 4][:, 0:512], onr[0:1, 0:128],
                                  bkrb, start=False, stop=True).then_inc(mma.h, 1)
                        mma.add(1)
                        if s_ == NS1 - 1:
                            te.drain().then_inc(mma.h, 1)
                            mma.add(1)
                    if s_ >= 1:
                        sk_ = s_ - 1
                        te.wait_ge(dva.h, dva_b + sk_ + 1)
                        for g in range(4):
                            te.matmul(
                                pb[4 + g // 2][:, (g % 2) * 130:(g % 2) * 130 + 129],
                                skv(sk_)[:, g * 128:(g + 1) * 128],
                                vaug(sk_)[:, g * 130:g * 130 + 129],
                                start=(sk_ == 0 and g % 2 == 0),
                                stop=(sk_ == NS1 - 1),
                                skip_group_check=True)
                        if sk_ == NS1 - 1:
                            te.drain().then_inc(mmb.h, 1)
                        else:
                            te.nop().then_inc(mmb.h, 1)
                        mmb.add(1)

            @blk.vector
            def _(v):
                ones4d = bfv(arF)[:, 0:16 * 520].rearrange(
                    "p (s g c) -> p s g c", g=4, c=130)[:, :, :, 128:130]
                chain(v, v.memset(ones4d, 1.0), dvs)
                for s_ in range(NS1):
                    v.wait_ge(mma.h, mma_b + min(s_ + 2, NS1 + 1))
                    dst = vaug(s_).rearrange("p (g c) -> p g c", c=130)[:, :, 0:128]
                    src = pb[s_ % 4][:].rearrange("p (g c) -> p g c", c=128)
                    v.tensor_copy(dst, src).then_inc(dva.h, 1)
                    dva.add(1)
                v.wait_ge(mmb.h, mmb_b + NS1)
                for g in range(4):
                    cp = v.tensor_copy(
                        kvaug[:, g * 130:g * 130 + 129],
                        pb[4 + g // 2][:, (g % 2) * 130:(g % 2) * 130 + 129])
                    if g == 3:
                        cp.then_inc(dva.h, 1)
                dva.add(1)

        dump("sk", bfv(arE)[:, 0:8192], (128, 8192), BF16)
        dump("vaug", bfv(arF)[:, 0:8320], (128, 8320), BF16)
        dump("kvaug", kvaug[:, :], (128, 520), BF16)

        _ck(3)
        def lhs_stream_phase(name, wdram, nin, nsl, rhs_fn, epi_act, epi_dve,
                            extra_gp=None, extra_mm_rhs=None,
                            act_reads_psum=False):
            with nc.Block(name) as blk:
                bases = dict(aca=aca.v, acb=acb.v,
                             mma=mma.v, dva=dva.v, dvb=dvb.v,
                             ldp=[s.v for s in ldp], ldq=[s.v for s in ldq],
                             ldx2=[s.v for s in ldx2])

                @blk.sync
                def _(sy):
                    k = 0
                    for osl in range(nsl):
                        for ic in range(nin):
                            if k >= 4:
                                sy.wait_ge(aca.h, bases["aca"] + (k - 3))
                            sem = ldp[k % 4]
                            sy.dma_start(stgv(k),
                                         wdram.ap()[ic * 128:(ic + 1) * 128,
                                                    osl * 256:osl * 256 + 256]
                                         ).then_inc(sem.h, 16)
                            sem.add(16)
                            k += 1

                if extra_gp is not None:
                    extra_gp(blk, bases)

                @blk.scalar
                def _(sc):
                    np_ = 0
                    for osl in range(nsl + 1):
                        if osl < nsl:
                            if osl >= 2:
                                sc.wait_ge(mma.h, bases["mma"] + 4 * (osl - 1))
                            for ic in range(nin):
                                sc.wait_ge(ldp[np_ % 4].h,
                                           bases["ldp"][np_ % 4]
                                           + 16 * (np_ // 4 + 1))
                                sc.activation(
                                    wslv(osl % 2)[:, ic * 256:(ic + 1) * 256],
                                    stgv(np_), AF.Copy).then_inc(aca.h, 1)
                                aca.add(1)
                                np_ += 1
                        if osl >= 1:
                            for q in range(4):
                                i = (osl - 1) * 4 + q
                                epi_act(sc, i, bases)

                @blk.tensor
                def _(te):
                    i = 0
                    for osl in range(nsl):
                        te.wait_ge(aca.h, bases["aca"] + nin * (osl + 1))
                        for k2 in range(2):
                            for th in range(2):
                                if i >= 4:
                                    te.wait_ge(dva.h, bases["dva"] + 2 * (i - 4) + 2)
                                    if act_reads_psum:
                                        te.wait_ge(acb.h,
                                                   bases["acb"] + 2 * (i - 4) + 1)
                                for ic in range(nin):
                                    mm = te.matmul(
                                        pb[i % 4][:, 0:512],
                                        wslv(osl % 2)[:, ic * 256 + k2 * 128:
                                                      ic * 256 + k2 * 128 + 128],
                                        rhs_fn(ic, th),
                                        start=(ic == 0),
                                        stop=(ic == nin - 1))
                                    if ic == nin - 1:
                                        mm.then_inc(mma.h, 1)
                                mma.add(1)
                                i += 1
                        if osl == nsl - 1:
                            te.drain().then_inc(mma.h, 1)
                            mma.add(1)

                @blk.vector
                def _(v):
                    for i in range(4 * nsl):
                        epi_dve(v, i, bases)
            return bases

        def q_epi_act(sc, i, bases):
            c = i // 2
            sc.wait_ge(mma.h, bases["mma"] + min(i + 2, 4 * 8 + 1))
            sc.activation(t1[i % 2], pb[i % 4][:], AF.Relu,
                          bias=bqs[:, c:c + 1]).then_inc(acb.h, 1)
            acb.add(1)
            sc.wait_ge(dva.h, bases["dva"] + 2 * i + 1)
            sc.activation(t3[i % 2], t2[i % 2], AF.Exp).then_inc(acb.h, 1)
            acb.add(1)

        def q_epi_dve(v, i, bases):
            c, th = i // 2, i % 2
            v.wait_ge(mma.h, bases["mma"] + min(i + 2, 4 * 8 + 1))
            v.wait_ge(acb.h, bases["acb"] + 2 * i + 1)
            v.tensor_scalar(t2[i % 2], pb[i % 4][:], bqs[:, c:c + 1], 0.0,
                            ALU.add, ALU.min).then_inc(dva.h, 1)
            dva.add(1)
            v.wait_ge(acb.h, bases["acb"] + 2 * i + 2)
            v.tensor_add(sqv(c)[:, th * 512:th * 512 + 512],
                         t1[i % 2], t3[i % 2]).then_inc(dva.h, 1)
            dva.add(1)

        lhs_stream_phase("qproj", wqt, NJ, 8,
                         lambda jc, th: xtb(jc)[:, th * 512:th * 512 + 512],
                         q_epi_act, q_epi_dve, act_reads_psum=True)

        dump("sqT", bfv(arB)[:, 0:16384], (128, 16384), BF16)

        _ck(4)
        with nc.Block("qkv") as blk:
            mma_b, mmb_b, dva_b, dvb_b = mma.v, mmb.v, dva.v, dvb.v

            @blk.tensor
            def _(te):
                for i in range(128):
                    h, ti = i // NT1, i % NT1
                    if i >= 4:
                        te.wait_ge(dva.h, dva_b + 2 * (i - 4) + 2)
                    te.matmul(pb[i % 4][:, 0:129],
                              sqv(h)[:, ti * 128:(ti + 1) * 128],
                              kvaug[:, (h // 4) * 130:(h // 4) * 130 + 129],
                              start=True, stop=True).then_inc(mma.h, 1)
                    mma.add(1)
                    if i >= 2:
                        k = i - 2
                        te.wait_ge(dva.h, dva_b + 2 * k + 2)
                        if k >= 2:
                            te.wait_ge(dvb.h, dvb_b + k - 1)
                        te.transpose(pbf[k % 2][:], tbx[k % 2][:, 0:128],
                                     idb[:]).then_inc(mmb.h, 1)
                        mmb.add(1)
                te.drain().then_inc(mma.h, 1)
                mma.add(1)
                for k in (126, 127):
                    te.wait_ge(dva.h, dva_b + 2 * k + 2)
                    te.wait_ge(dvb.h, dvb_b + k - 1)
                    te.transpose(pbf[k % 2][:], tbx[k % 2][:, 0:128], idb[:]
                                 ).then_inc(mmb.h, 1)
                    mmb.add(1)
                te.drain().then_inc(mmb.h, 1)
                mmb.add(1)

            @blk.vector
            def _(v):
                for i in range(128):
                    v.wait_ge(mma.h, mma_b + min(i + 3, 129))
                    v.nop().then_inc(dva.h, 1)
                    dva.add(1)
                    chain(v, v.tensor_scalar(rc[4][:, 0:1],
                                             pb[i % 4][:, 128:129],
                                             EPS_ATTN, None, ALU.add), dvs)
                    chain(v, v.reciprocal(rc[5][:, 0:1], rc[4][:, 0:1]), dvs)
                    v.tensor_scalar(tbx[i % 2][:, 0:128], pb[i % 4][:, 0:128],
                                    rc[5][:, 0:1], None, ALU.mult).then_inc(dva.h, 1)
                    dva.add(1)
                    if i >= 1:
                        k = i - 1
                        h, ti = k // NT1, k % NT1
                        v.wait_ge(mmb.h, mmb_b + min(k + 2, 129))
                        v.tensor_copy(atv(h)[:, ti * 128:(ti + 1) * 128],
                                      pbf[k % 2][:]).then_inc(dvb.h, 1)
                        dvb.add(1)
                v.wait_ge(mmb.h, mmb_b + 129)
                v.tensor_copy(atv(15)[:, 7 * 128:8 * 128], pbf[127 % 2][:]
                              ).then_inc(dvb.h, 1)
                dvb.add(1)

        dump("attnT", bfv(arC)[:, 0:16384], (128, 16384), BF16)
        dump("qdbg", stg[:, 0:512], (128, 512), F32)

        _ck(5)
        def wo_extra_gp(blk, bases):
            @blk.gpsimd
            def _(gp):
                for oc in range(NJ):
                    if oc < 2:
                        gp.wait_ge(dvs.h, dvs_init_done)
                    else:
                        gp.wait_ge(dva.h, bases["dva"] + 2 * (2 * (oc - 2) + 1) + 2)
                    sem = ldx2[oc % 2]
                    gp.dma_start(w2st[oc % 2][:, :],
                                 xt.ap()[oc * 128:(oc + 1) * 128, 0:1024]
                                 ).then_inc(sem.h, 16)
                    sem.add(16)

        def wo_epi_act(sc, i, bases):
            pass

        def wo_epi_dve(v, i, bases):
            oc, th = i // 2, i % 2
            v.wait_ge(mma.h, bases["mma"] + min(i + 2, 4 * 8 + 1))
            v.wait_ge(ldx2[oc % 2].h, bases["ldx2"][oc % 2] + 16 * (oc // 2 + 1))
            v.tensor_add(y1v(oc)[:, th * 512:th * 512 + 512], pb[i % 4][:],
                         w2st[oc % 2][:, th * 512:th * 512 + 512]
                         ).then_inc(dva.h, 2)
            dva.add(2)

        lhs_stream_phase("wo", wot, NJ, 8,
                         lambda fc, th: atv(fc)[:, th * 512:th * 512 + 512],
                         wo_epi_act, wo_epi_dve, extra_gp=wo_extra_gp)

        def ln_phase(name, also_bf16_copy, final_store, ready_off):
            with nc.Block(name) as blk:
                mma_b, mmb_b, dva_b, dvb_b, aca_b = mma.v, mmb.v, dva.v, dvb.v, aca.v
                stf_b = stf.v

                @blk.vector
                def _(v):
                    for th in range(2):
                        v.wait_ge(mma.h, mma_b + 16 * (th + 1))
                        chain(v, v.tensor_scalar(r_m, pb[4 + th][0:1, 0:512],
                                                 1.0 / J, None, ALU.mult), dvs)
                        chain(v, v.tensor_tensor(r_t, r_m, r_m, ALU.mult), dvs)
                        chain(v, v.scalar_tensor_tensor(r_t, pbx[th][0:1, 0:512],
                                                        1.0 / J, r_t, ALU.mult,
                                                        ALU.subtract), dvs)
                        v.tensor_scalar(r_t, r_t, EPS_LN, None, ALU.add
                                        ).then_inc(dvb.h, 1)
                        dvb.add(1)
                        v.wait_ge(aca.h, aca_b + 17 * (th + 1))
                        chain(v, v.reciprocal(r_r, r_mr), dvs)
                        chain(v, v.tensor_tensor(r_mr, r_m, r_r, ALU.mult), dvs)
                        v.tensor_copy(r_rb, r_r)
                        v.tensor_copy(r_mrb, r_mr).then_inc(dvb.h, 1)
                        dvb.add(1)
                        v.wait_ge(mmb.h, mmb_b + 2 * (th + 1))
                        for oc in range(NJ):
                            sl = y1v(oc)[:, th * 512:th * 512 + 512]
                            chain(v, v.tensor_tensor(t2[oc % 2], sl, pb[0][:],
                                                     ALU.mult), dvs)
                            chain(v, v.tensor_tensor(sl, t2[oc % 2], pb[1][:],
                                                     ALU.subtract), dvs)
                            if also_bf16_copy:
                                v.tensor_copy(x2bv(oc)[:, th * 512:th * 512 + 512],
                                              sl)
                            v.nop().then_inc(dvb.h, 1)
                            dvb.add(1)

                @blk.scalar
                def _(sc):
                    i = 0
                    for th in range(2):
                        for oc in range(NJ):
                            if i >= 2:
                                sc.wait_ge(mma.h, mma_b + (i - 2) + 1)
                            sc.wait_ge(dva.h, dva_b + ready_off(th, oc))
                            sl = y1v(oc)[:, th * 512:th * 512 + 512]
                            sc.activation(tbx[i % 2][:], sl, AF.Copy)
                            sc.activation(t1[i % 2].bitcast(BF16)[:, 0:512],
                                          sl, AF.Square).then_inc(aca.h, 1)
                            aca.add(1)
                            i += 1
                        sc.wait_ge(dvb.h, dvb_b + 18 * th + 1)
                        sc.activation(r_mr, r_t, AF.Sqrt
                                      ).then_inc(aca.h, 1)
                        aca.add(1)

                @blk.tensor
                def _(te):
                    i = 0
                    for th in range(2):
                        for oc in range(NJ):
                            te.wait_ge(aca.h, aca_b + i + 1 + th)
                            te.matmul(pb[4 + th][0:1, 0:512], onc[:, 0:1],
                                      tbx[i % 2][:], start=(oc == 0),
                                      stop=(oc == NJ - 1))
                            mm = te.matmul(pbx[th][0:1, 0:512], onc[:, 0:1],
                                           t1[i % 2].bitcast(BF16)[:, 0:512],
                                           start=(oc == 0), stop=(oc == NJ - 1))
                            if oc == NJ - 1:
                                te.drain().then_inc(mma.h, 1)
                            else:
                                mm.then_inc(mma.h, 1)
                            mma.add(1)
                            i += 1
                    for th in range(2):
                        te.wait_ge(dvb.h, dvb_b + 18 * th + 2)
                        te.matmul(pb[0][:, 0:512], onr[0:1, 0:128],
                                  r_rb[0:1, 0:512], start=True, stop=True)
                        te.drain().then_inc(mmb.h, 1)
                        mmb.add(1)
                        te.matmul(pb[1][:, 0:512], onr[0:1, 0:128],
                                  r_mrb[0:1, 0:512], start=True, stop=True)
                        te.drain().then_inc(mmb.h, 1)
                        mmb.add(1)

                if final_store:
                    @blk.sync
                    def _(sy):
                        for oc in range(NJ):
                            sy.wait_ge(dvb.h, dvb_b + 18 + 2 + (oc + 1))
                            sy.dma_start(out.ap()[oc * 128:(oc + 1) * 128, :],
                                         y1v(oc)).then_inc(stf.h, 16)
                            stf.add(16)
                        sy.wait_ge(stf.h, stf_b + 16 * NJ)

        _ck(6)
        dump("y1", arA[:, :], (128, 16384), F32)
        ln_phase("ln1", also_bf16_copy=True, final_store=False,
                 ready_off=lambda th, oc: -62 + 4 * oc + 2 * th)
        dump("x2", arA[:, :], (128, 16384), F32)

        _ck(7)
        with nc.Block("routerdown") as blk:
            RD = dict(mma=mma.v, mmb=mmb.v, mmc=mmc.v, dva=dva.v, dvb=dvb.v,
                      aca=aca.v, acb=acb.v, acs=acs.v,
                      ldp=[s.v for s in ldp])

            @blk.sync
            def _(sy):
                k = 0
                for osl in range(4):
                    for ic in range(NJ):
                        if k >= 4:
                            sy.wait_ge(aca.h, RD["aca"] + (k - 3))
                        sem = ldp[k % 4]
                        sy.dma_start(stgv(k),
                                     down_w.ap()[ic * 128:(ic + 1) * 128,
                                                 osl * 256:osl * 256 + 256]
                                     ).then_inc(sem.h, 16)
                        sem.add(16)
                        k += 1

            @blk.tensor
            def _(te):
                for th in range(2):
                    te.wait_ge(dvb.h, RD["dvb"] - 36 + 18 * th + 2)
                    te.wait_ge(dvb.h, RD["dvb"] - 18 * (1 - th))
                    for jc in range(NJ):
                        te.matmul(pb[4 + th][0:8, 0:512],
                                  rws[:, jc * 8:(jc + 1) * 8],
                                  y1v(jc)[:, th * 512:th * 512 + 512],
                                  start=(jc == 0), stop=(jc == NJ - 1))
                    te.drain().then_inc(mma.h, 1)
                    mma.add(1)
                te.wait_ge(dva.h, RD["dva"] + 1)
                for ti in range(NT1):
                    te.transpose(pb[2][:, ti * 8:ti * 8 + 8],
                                 logit[0:8, ti * 128:(ti + 1) * 128],
                                 id8[0:8, 0:8])
                te.drain().then_inc(mma.h, 1)
                mma.add(1)
                te.wait_ge(dvb.h, RD["dvb"])
                i = 0
                dbank = [0, 1, 3]
                for osl in range(4):
                    te.wait_ge(aca.h, RD["aca"] + NJ * (osl + 1))
                    for k2 in range(2):
                        for th in range(2):
                            if i >= 3:
                                te.wait_ge(acb.h, RD["acb"] + (i - 3) + 1)
                            for ic in range(NJ):
                                mm = te.matmul(
                                    pb[dbank[i % 3]][:, 0:512],
                                    wslv(osl % 2)[:, ic * 256 + k2 * 128:
                                                  ic * 256 + k2 * 128 + 128],
                                    x2bv(ic)[:, th * 512:th * 512 + 512],
                                    start=(ic == 0), stop=(ic == NJ - 1))
                                if ic == NJ - 1:
                                    mm.then_inc(mma.h, 1)
                            mma.add(1)
                            i += 1
                    if osl == 3:
                        te.drain().then_inc(mma.h, 1)
                        mma.add(1)

            @blk.scalar
            def _(sc):
                np_ = 0
                for osl in range(5):
                    if osl < 4:
                        if osl >= 2:
                            sc.wait_ge(mma.h, RD["mma"] + 3 + 4 * (osl - 1))
                        for ic in range(NJ):
                            sc.wait_ge(ldp[np_ % 4].h,
                                       RD["ldp"][np_ % 4]
                                       + 16 * (np_ // 4 + 1))
                            sc.activation(
                                wslv(osl % 2)[:, ic * 256:(ic + 1) * 256],
                                stgv(np_), AF.Copy).then_inc(aca.h, 1)
                            aca.add(1)
                            np_ += 1
                    if osl >= 1:
                        for q in range(4):
                            i = (osl - 1) * 4 + q
                            lc, th = i // 2, i % 2
                            sc.wait_ge(mma.h, RD["mma"] + 3 + min(i + 2, 17))
                            sc.activation(gTv(lc)[:, th * 512:th * 512 + 512],
                                          pb[[0, 1, 3][i % 3]][:], AF.Gelu,
                                          bias=dbs[:, lc:lc + 1]
                                          ).then_inc(acb.h, 1)
                            acb.add(1)
                for ti in range(NT1):
                    sc.wait_ge(dvb.h, RD["dvb"] + 3 * ti + 1)
                    sc.activation(rt[1][:], rt[0][:], AF.Exp, bias=rc[1][:, 0:1]
                                  ).then_inc(acs.h, 1)
                    acs.add(1)
                    sc.wait_ge(dvb.h, RD["dvb"] + 3 * ti + 2)
                    sc.activation(rc[3][:, 0:1], rc[3][:, 0:1], AF.Sigmoid)
                    sc.activation(rc[4][:, 0:1], rc[4][:, 0:1], AF.Sigmoid
                                  ).then_inc(acs.h, 1)
                    acs.add(1)

            @blk.vector
            def _(v):
                for th in range(2):
                    v.wait_ge(mma.h, RD["mma"] + th + 1)
                    ts = v.tensor_scalar(logit[0:8, th * 512:th * 512 + 512],
                                         pb[4 + th][0:8, 0:512], rbc[0:8, 0:1],
                                         ncc[0:8, 0:1], ALU.add, ALU.add)
                    if th == 1:
                        ts.then_inc(dva.h, 1)
                dva.add(1)
                v.wait_ge(mma.h, RD["mma"] + 3)
                for ti in range(NT1):
                    lg = rt[0]
                    chain(v, v.tensor_copy(lg[:], pb[2][:, ti * 8:ti * 8 + 8]),
                          dvs)
                    chain(v, v.reduce_max(rc[0][:, 0:1], lg[:], axis=AX.X), dvs)
                    v.tensor_scalar(rc[1][:, 0:1], rc[0][:, 0:1], -1.0, None,
                                    ALU.mult).then_inc(dvb.h, 1)
                    dvb.add(1)
                    v.wait_ge(acs.h, RD["acs"] + 2 * ti + 1)
                    chain(v, v.reduce_sum(rc[2][:, 0:1], rt[1][:], axis=AX.X),
                          dvs)
                    chain(v, v.reciprocal(rc[2][:, 0:1], rc[2][:, 0:1]), dvs)
                    chain(v, v.tensor_scalar(rt[1][:], rt[1][:], rc[2][:, 0:1],
                                             None, ALU.mult), dvs)
                    chain(v, v.tensor_single_scalar(rt[2][:], lg[:],
                                                    rc[0][:, 0:1], ALU.is_ge),
                          dvs)
                    chain(v, v.scalar_tensor_tensor(rt[3][:], rt[2][:], -1e4,
                                                    lg[:], ALU.mult, ALU.add),
                          dvs)
                    chain(v, v.reduce_max(rc[3][:, 0:1], rt[3][:], axis=AX.X),
                          dvs)
                    chain(v, v.tensor_single_scalar(rt[3][:], rt[3][:],
                                                    rc[3][:, 0:1], ALU.is_ge),
                          dvs)
                    chain(v, v.reduce_max(rc[4][:, 0:1], rt[1][:], axis=AX.X),
                          dvs)
                    chain(v, v.scalar_tensor_tensor(rt[0][:], rt[2][:], -2.0,
                                                    rt[1][:], ALU.mult,
                                                    ALU.add), dvs)
                    chain(v, v.reduce_max(rc[5][:, 0:1], rt[0][:], axis=AX.X),
                          dvs)
                    chain(v, v.tensor_tensor(rc[3][:, 0:1], rc[4][:, 0:1],
                                             rc[5][:, 0:1], ALU.subtract), dvs)
                    v.tensor_scalar(rc[4][:, 0:1], rc[3][:, 0:1], -1.0, None,
                                    ALU.mult).then_inc(dvb.h, 1)
                    dvb.add(1)
                    v.wait_ge(acs.h, RD["acs"] + 2 * ti + 2)
                    chain(v, v.tensor_scalar(rt[2][:], rt[2][:], rc[3][:, 0:1],
                                             None, ALU.mult), dvs)
                    chain(v, v.scalar_tensor_tensor(wgt[:, ti * 8:(ti + 1) * 8],
                                                    rt[3][:], rc[4][:, 0:1],
                                                    rt[2][:], ALU.mult,
                                                    ALU.add), dvs)
                    v.tensor_copy(wbt[:, ti * 8:(ti + 1) * 8],
                                  wgt[:, ti * 8:(ti + 1) * 8]).then_inc(dvb.h, 1)
                    dvb.add(1)

        _ck(9)
        with nc.Block("experts") as blk:
            F2 = dict(gpc=gpc.v, aca=aca.v, acb=acb.v,
                      mma=mma.v, mmb=mmb.v, mmc=mmc.v, dva=dva.v, dvb=dvb.v,
                      ldp=[s.v for s in ldp], ldq=[s.v for s in ldq])
            eb1 = [pb[0], pb[1], pbx[0]]
            eb2 = [pb[2], pb[3], pbx[1]]
            sch = []
            pz = 0
            pm = 0
            psl = 0
            pw2 = 0
            for gi, g in enumerate(GROUPS):
                nsl = g["nch"] // 2
                sch.append(dict(g=g, gi=gi, nsl=nsl, z0=pz, m0=pm, sl0=psl,
                                w0=pw2, zd0=pz + gi, md0=pm + gi))
                pz += 2 * g["nch"]
                pm += 16
                psl += nsl
                pw2 += g["nch"]
            TOT_Z, TOT_M, TOT_SL, TOT_W2 = pz, pm, psl, pw2
            slice_end_mma = []
            for e_ in sch:
                for s_ in range(e_["nsl"]):
                    c = e_["zd0"] + 4 * (s_ + 1)
                    if s_ == e_["nsl"] - 1:
                        c += 1
                    slice_end_mma.append(c)

            @blk.sync
            def _(sy):
                npiece = 0
                for e_ in sch:
                    g = e_["g"]
                    wsrc = shared_w1 if g["shared"] else w1d[g["e"]]
                    w2src = shared_w2 if g["shared"] else w2d[g["e"]]
                    pig = 0
                    for s_ in range(e_["nsl"]):
                        col0 = (g["hc0"] // 2 + s_) * 256
                        for lc in range(8):
                            if npiece >= 4:
                                sy.wait_ge(gpc.h, F2["gpc"] + (npiece - 3))
                            sem = ldp[npiece % 4]
                            sy.dma_start(stgv(npiece),
                                         wsrc.ap()[lc * 128:(lc + 1) * 128,
                                                   col0:col0 + 256]
                                         ).then_inc(sem.h, 16)
                            sem.add(16)
                            npiece += 1
                            pig += 1
                            if pig % 4 == 0:
                                hcl = pig // 4 - 1
                                kc = e_["w0"] + hcl
                                if kc >= 2:
                                    sy.wait_ge(aca.h, F2["aca"] + (kc - 1))
                                r0 = (g["hc0"] + hcl) * 128
                                sem = ldq[kc % 2]
                                sy.dma_start(w2st[kc % 2][:, :],
                                             w2src.ap()[r0:r0 + 128, :]
                                             ).then_inc(sem.h, 16)
                                sem.add(16)

            @blk.gpsimd
            def _(gp):
                npiece = 0
                for e_ in sch:
                    for s_ in range(e_["nsl"]):
                        sg = e_["sl0"] + s_
                        if sg >= 2:
                            gp.wait_ge(mma.h,
                                       F2["mma"] + slice_end_mma[sg - 2])
                        for lc in range(8):
                            gp.wait_ge(ldp[npiece % 4].h,
                                       F2["ldp"][npiece % 4]
                                       + 16 * (npiece // 4 + 1))
                            gp.tensor_copy(
                                wslv(sg % 2)[:, lc * 256:(lc + 1) * 256],
                                stgv(npiece)).then_inc(gpc.h, 1)
                            gpc.add(1)
                            npiece += 1

            @blk.scalar
            def _(sc):
                for e_ in sch:
                    g = e_["g"]
                    sc.wait_ge(mmc.h, F2["mmc"] + e_["gi"])
                    for zt in range(2 * g["nch"]):
                        hcl, th = zt // 2, zt % 2
                        ztg = e_["z0"] + zt
                        bcol = g["gb"] + g["hc0"] + hcl
                        sc.wait_ge(mma.h, F2["mma"] + e_["zd0"]
                                   + min(zt + 2, 2 * g["nch"] + 1))
                        sc.activation(Uv(hcl)[:, th * 512:th * 512 + 512],
                                      eb1[ztg % 3][:], AF.Gelu,
                                      bias=b1all[:, bcol:bcol + 1]
                                      ).then_inc(acb.h, 1)
                        acb.add(1)
                        if zt < g["nch"]:
                            kc = e_["w0"] + zt
                            sc.wait_ge(ldq[kc % 2].h,
                                       F2["ldq"][kc % 2] + 16 * (kc // 2 + 1))
                            sc.activation(w2g(zt)[:, :], w2st[kc % 2][:, :],
                                          AF.Copy).then_inc(aca.h, 1)
                            aca.add(1)

            @blk.tensor
            def _(te):
                first_ph = True
                first_tile = True
                for e_ in sch:
                    g = e_["g"]
                    for s_ in range(e_["nsl"]):
                        sg = e_["sl0"] + s_
                        te.wait_ge(gpc.h, F2["gpc"] + 8 * (sg + 1))
                        for k2 in range(2):
                            for th in range(2):
                                zl = (s_ * 2 + k2) * 2 + th
                                ztg = e_["z0"] + zl
                                if first_tile:
                                    te.wait_ge(dvb.h, F2["dvb"])
                                    first_tile = False
                                if ztg >= 3:
                                    te.wait_ge(acb.h, F2["acb"] + ztg - 2)
                                for lc in range(8):
                                    mm = te.matmul(
                                        eb1[ztg % 3][:, 0:512],
                                        wslv(sg % 2)[:, lc * 256 + k2 * 128:
                                                     lc * 256 + k2 * 128 + 128],
                                        gTv(lc)[:, th * 512:th * 512 + 512],
                                        start=(lc == 0), stop=(lc == 7))
                                    if lc == 7:
                                        mm.then_inc(mma.h, 1)
                                mma.add(1)
                                if zl == 2 * g["nch"] - 1:
                                    te.drain().then_inc(mma.h, 1)
                                    mma.add(1)
                    te.wait_ge(acb.h, F2["acb"] + e_["z0"] + 2 * g["nch"])
                    te.wait_ge(aca.h, F2["aca"] + e_["w0"] + g["nch"])
                    for ti in range(NT1):
                        for lh in range(2):
                            ml = ti * 2 + lh
                            mtg = e_["m0"] + ml
                            if mtg >= 3:
                                te.wait_ge(dva.h, F2["dva"] + mtg - 2)
                            for hcl in range(g["nch"]):
                                mm = te.matmul(eb2[mtg % 3][:, 0:512],
                                               Uv(hcl)[:, ti * 128:(ti + 1) * 128],
                                               w2g(hcl)[:, lh * 512:lh * 512 + 512],
                                               start=(hcl == 0),
                                               stop=(hcl == g["nch"] - 1))
                                if hcl == g["nch"] - 1:
                                    mm.then_inc(mmb.h, 1)
                            mmb.add(1)
                            if ml == 15:
                                te.drain().then_inc(mmb.h, 1)
                                mmb.add(1)
                    if not g["shared"]:
                        nb = sum(1 for x in sch[:e_["gi"] + 1]
                                 if not x["g"]["shared"])
                        te.wait_ge(dvb.h, F2["dvb"] + nb)
                        for hcl in range(g["nch"]):
                            hcg = g["gb"] + g["hc0"] + hcl
                            last = (hcg == N_PHANT - 1)
                            for lh in range(2):
                                mm = te.matmul(pb[4 + lh][0:8, 0:512],
                                               phg[:, hcl * 8:hcl * 8 + 8],
                                               w2g(hcl)[:, lh * 512:lh * 512 + 512],
                                               start=first_ph, stop=last,
                                               skip_group_check=True)
                                if hcl == g["nch"] - 1 and lh == 1:
                                    mm.then_inc(mmc.h, 1)
                            first_ph = False
                    else:
                        te.nop().then_inc(mmc.h, 1)
                    mmc.add(1)

            @blk.vector
            def _(v):
                ph3 = phg[:].rearrange("p (c k) -> p c k", k=8)
                for e_ in sch:
                    g = e_["g"]
                    if not g["shared"]:
                        v.wait_ge(mmc.h, F2["mmc"] + e_["gi"])
                        chain(v, v.memset(phg[:], 0.0), dvs)
                        b0 = g["gb"] + g["hc0"]
                        v.tensor_copy(ph3[:, 0:g["nch"], g["e"]:g["e"] + 1],
                                      gb1all[:, b0:b0 + g["nch"]].unsqueeze(2)
                                      ).then_inc(dvb.h, 1)
                        dvb.add(1)
                    for ti in range(NT1):
                        for lh in range(2):
                            ml = ti * 2 + lh
                            mtg = e_["m0"] + ml
                            v.wait_ge(mmb.h, F2["mmb"] + e_["md0"]
                                      + min(ml + 2, 17))
                            dst = accv(ti)[:, lh * 512:lh * 512 + 512]
                            if g["shared"]:
                                fl = v.scalar_tensor_tensor(
                                    dst, eb2[mtg % 3][:],
                                    0.1, dst, ALU.mult, ALU.add)
                            elif g["e"] == 0 and g["hc0"] == 0:
                                fl = v.tensor_scalar(dst, eb2[mtg % 3][:],
                                                     wgt[:, ti * 8 + g["e"]:
                                                         ti * 8 + g["e"] + 1],
                                                     None, ALU.mult)
                            else:
                                fl = v.scalar_tensor_tensor(
                                    dst, eb2[mtg % 3][:],
                                    wgt[:, ti * 8 + g["e"]:
                                        ti * 8 + g["e"] + 1],
                                    dst, ALU.mult, ALU.add)
                            fl.then_inc(dva.h, 1)
                            dva.add(1)

        _ck(10)
        with nc.Block("corr") as blk:
            mma_b, mmb_b, dva_b, dvb_b, aca_b = mma.v, mmb.v, dva.v, dvb.v, aca.v
            ldb_b = ldb.v
            cb3 = [pb[0], pb[1], pb[2]]
            mmc_b = mmc.v

            @blk.sync
            def _(sy):
                sy.dma_start(b2f, b2all.ap()).then_inc(ldb.h, 16)
                ldb.add(16)
                sy.dma_start(b2sf, shared_b2_row.ap()).then_inc(ldb.h, 16)
                ldb.add(16)

            @blk.vector
            def _(v):
                v.wait_ge(ldb.h, ldb_b + 32)
                v.tensor_copy(b2bf, b2f)
                v.tensor_copy(tbx[0][0:1, 0:512], b2sf[:, 0:512])
                v.tensor_copy(tbx[1][0:1, 0:512], b2sf[:, 512:1024])
                v.tensor_scalar(rows17[0:8, 0:512],
                                pb[4][0:8, 0:512], -1.0, None, ALU.mult)
                v.tensor_scalar(rows17[0:8, 512:1024],
                                pb[5][0:8, 0:512], -1.0, None, ALU.mult
                                ).then_inc(dva.h, 1)
                dva.add(1)
                for ti in range(NT1):
                    v.wait_ge(mmc.h, mmc_b + ti + 1)
                    v.tensor_copy(waug[0:8, ti * 128:(ti + 1) * 128],
                                  pbf[ti % 2][0:8, 0:128]).then_inc(dvb.h, 1)
                    dvb.add(1)
                for ti in range(NT1):
                    for lh in range(2):
                        i = ti * 2 + lh
                        v.wait_ge(mmb.h, mmb_b + min(i + 2, 17))
                        dst = accv(ti)[:, lh * 512:lh * 512 + 512]
                        v.tensor_tensor(dst, dst, cb3[i % 3][:], ALU.add
                                        ).then_inc(dvb.h, 1)
                        dvb.add(1)

            @blk.tensor
            def _(te):
                te.wait_ge(dva.h, dva_b + 1)
                for lh in range(2):
                    te.matmul(pb[2 + lh][0:1, 0:512], sgn[0:8, 0:1],
                              rows17[0:8, lh * 512:lh * 512 + 512],
                              start=True, stop=False)
                    te.matmul(pb[2 + lh][0:1, 0:512], onc[0:8, 0:1],
                              b2bf[:, lh * 512:lh * 512 + 512],
                              start=False, stop=False)
                    te.matmul(pb[2 + lh][0:1, 0:512], p01[0:1, 0:1],
                              tbx[lh][0:1, 0:512], start=False, stop=True)
                te.drain().then_inc(mma.h, 1)
                mma.add(1)
                te.wait_ge(aca.h, aca_b + 1)
                for ti in range(NT1):
                    if ti >= 2:
                        te.wait_ge(dvb.h, dvb_b + (ti - 2) + 1)
                    te.transpose(pbf[ti % 2][0:8, 0:128],
                                 wbt[:, ti * 8:ti * 8 + 8], idb[:])
                    te.drain().then_inc(mmc.h, 1)
                    mmc.add(1)
                te.wait_ge(dvb.h, dvb_b + 8)
                for ti in range(NT1):
                    for lh in range(2):
                        i = ti * 2 + lh
                        if i >= 3:
                            te.wait_ge(dvb.h, dvb_b + 8 + i - 2)
                        te.matmul(cb3[i % 3][:, 0:512],
                                  waug[0:8, ti * 128:(ti + 1) * 128],
                                  rows17[0:8, lh * 512:lh * 512 + 512],
                                  start=True, stop=False)
                        te.matmul(cb3[i % 3][:, 0:512], onr[0:1, 0:128],
                                  wslv(0)[0:1, lh * 512:lh * 512 + 512],
                                  start=False, stop=True).then_inc(mmb.h, 1)
                        mmb.add(1)
                        if i == 15:
                            te.drain().then_inc(mmb.h, 1)
                            mmb.add(1)

            @blk.scalar
            def _(sc):
                sc.wait_ge(mma.h, mma_b + 1)
                for lh in range(2):
                    act = sc.activation(wslv(0)[0:1, lh * 512:lh * 512 + 512],
                                        pb[2 + lh][0:1, 0:512], AF.Copy)
                    if lh == 1:
                        act.then_inc(aca.h, 1)
                aca.add(1)

        dump("acc", arC[:, :], (128, 8192), F32)
        dump("rows17", rows17[0:8, :], (8, 1024), BF16)
        dump("cmb", wslv(0)[0:1, 0:1024], (1, 1024), BF16)

        _ck(11)
        with nc.Block("trans") as blk:
            mma_b, dva_b, dvb_b = mma.v, dva.v, dvb.v

            @blk.vector
            def _(v):
                def cast(j):
                    ti, lh = j // 2, j % 2
                    if j >= 2:
                        v.wait_ge(mma.h, mma_b + 4 * (j - 1))
                    v.tensor_copy(tbx[j % 2][:],
                                  accv(ti)[:, lh * 512:lh * 512 + 512]
                                  ).then_inc(dva.h, 1)
                    dva.add(1)

                cast(0)
                cast(1)
                for i in range(16):
                    ti, lh = i // 2, i % 2
                    for q in range(4):
                        k = i * 4 + q
                        if q == 3 and i + 2 < 16:
                            cast(i + 2)
                        v.wait_ge(mma.h, mma_b + min(k + 2, 66))
                        lc = lh * 4 + q
                        v.tensor_copy(moev(lc)[:, ti * 128:(ti + 1) * 128],
                                      pbf[k % 2][:]).then_inc(dvb.h, 1)
                        dvb.add(1)

            @blk.tensor
            def _(te):
                for i in range(16):
                    ti, lh = i // 2, i % 2
                    te.wait_ge(dva.h, dva_b + i + 1)
                    for q in range(4):
                        k = i * 4 + q
                        if k >= 2:
                            te.wait_ge(dvb.h, dvb_b + k - 1)
                        te.transpose(pbf[k % 2][:],
                                     tbx[i % 2][:, q * 128:q * 128 + 128],
                                     idb[:]).then_inc(mma.h, 1)
                        mma.add(1)
                if True:
                    te.drain().then_inc(mma.h, 2)
                    mma.add(2)

        dump("moeT", bfv(arF)[:, 0:8192], (128, 8192), BF16)

        _ck(12)
        with nc.Block("upcore") as blk:
            UC = dict(aca=aca.v, acb=acb.v, mma=mma.v, dva=dva.v,
                      ldp=[s.v for s in ldp], ldq=[s.v for s in ldq])

            @blk.sync
            def _(sy):
                npiece = 0
                nup = 0
                for osl in range(8):
                    for jc in range(NJ):
                        if npiece >= 4:
                            sy.wait_ge(aca.h, UC["aca"] + (npiece - 3))
                        sem = ldp[npiece % 4]
                        sy.dma_start(stgv(npiece),
                                     core_w.ap()[jc * 128:(jc + 1) * 128,
                                                 osl * 256:osl * 256 + 256]
                                     ).then_inc(sem.h, 16)
                        sem.add(16)
                        npiece += 1
                    for lc in range(NL):
                        if nup >= 2:
                            sy.wait_ge(acb.h, UC["acb"] + nup - 1)
                        sem = ldq[nup % 2]
                        sy.dma_start(w2st[nup % 2][:, 0:256],
                                     up_w.ap()[lc * 128:(lc + 1) * 128,
                                               osl * 256:osl * 256 + 256]
                                     ).then_inc(sem.h, 16)
                        sem.add(16)
                        nup += 1

            @blk.scalar
            def _(sc):
                for jc in range(NJ):
                    sc.activation(gx2v(jc)[:, :], y1v(jc), AF.Gelu)
                npiece = 0
                nup = 0
                for osl in range(8):
                    if osl >= 2:
                        sc.wait_ge(mma.h, UC["mma"] + 4 * (osl - 1))
                    for jc in range(NJ):
                        sc.wait_ge(ldp[npiece % 4].h,
                                   UC["ldp"][npiece % 4] + 16 * (npiece // 4 + 1))
                        sc.activation(wslv(osl % 2)[:, jc * 256:(jc + 1) * 256],
                                      stgv(npiece), AF.Copy).then_inc(aca.h, 1)
                        aca.add(1)
                        npiece += 1
                    for lc in range(NL):
                        sc.wait_ge(ldq[nup % 2].h,
                                   UC["ldq"][nup % 2] + 16 * (nup // 2 + 1))
                        sc.activation(upslv(osl % 2)[:, lc * 256:(lc + 1) * 256],
                                      w2st[nup % 2][:, 0:256], AF.Copy
                                      ).then_inc(acb.h, 1)
                        acb.add(1)
                        nup += 1

            @blk.vector
            def _(v):
                chain(v, v.tensor_add(bsum[:, :], ubs[:, :], cbs[:, :]), dvs)
                v.nop().then_inc(dva.h, 1)
                dva.add(1)
                for i in range(32):
                    oc, th = i // 2, i % 2
                    v.wait_ge(mma.h, UC["mma"] + min(i + 2, 33))
                    sl = y1v(oc)[:, th * 512:th * 512 + 512]
                    v.scalar_tensor_tensor(sl, pb[i % 4][:],
                                           bsum[:, oc:oc + 1], sl, ALU.add,
                                           ALU.add).then_inc(dva.h, 1)
                    dva.add(1)

            @blk.tensor
            def _(te):
                i = 0
                for osl in range(8):
                    te.wait_ge(aca.h, UC["aca"] + NJ * (osl + 1))
                    te.wait_ge(acb.h, UC["acb"] + NL * (osl + 1))
                    for k2 in range(2):
                        for th in range(2):
                            if i >= 4:
                                te.wait_ge(dva.h, UC["dva"] + (i - 4) + 2)
                            for jc in range(NJ):
                                te.matmul(
                                    pb[i % 4][:, 0:512],
                                    wslv(osl % 2)[:, jc * 256 + k2 * 128:
                                                  jc * 256 + k2 * 128 + 128],
                                    gx2v(jc)[:, th * 512:th * 512 + 512],
                                    start=(jc == 0), stop=False)
                            for lc in range(NL):
                                mm = te.matmul(
                                    pb[i % 4][:, 0:512],
                                    upslv(osl % 2)[:, lc * 256 + k2 * 128:
                                                   lc * 256 + k2 * 128 + 128],
                                    moev(lc)[:, th * 512:th * 512 + 512],
                                    start=False, stop=(lc == NL - 1))
                                if lc == NL - 1:
                                    mm.then_inc(mma.h, 1)
                            mma.add(1)
                            i += 1
                    if osl == 7:
                        te.drain().then_inc(mma.h, 1)
                        mma.add(1)

        _ck(13)
        dump("y2", arA[:, :], (128, 16384), F32)

        ln_phase("ln2", also_bf16_copy=False, final_store=True,
                 ready_off=lambda th, oc: -31 + 2 * oc + th)


    except _StopBuild:
        pass

    stx.close()
    nc.finalize()
    return nc


def _build_guarded():
    return build_nc()


_cost = np.float32(np.array([2 * 1024 * h for h in HE], dtype=np.float32)
                   * np.float32(1e-7))


def _stage_weights(inp):
    f = np.ascontiguousarray
    m = {}
    m["wqt"] = f(inp["L0_attn_Wq"].T)
    m["wkt"] = f(inp["L0_attn_Wk"].T)
    m["wvt"] = f(inp["L0_attn_Wv"].T)
    m["wot"] = f(inp["L0_attn_Wo"].T)
    m["bq_t"] = f(inp["L0_attn_bq"].reshape(NJ, 128).T)
    m["bk_row"] = f(inp["L0_attn_bk"].reshape(1, 512))
    m["bv_row"] = f(inp["L0_attn_bv"].reshape(1, 512))
    m["down_w"] = f(inp["L0_ffn_down_w"])
    m["down_b_t"] = f(inp["L0_ffn_down_b"].reshape(NL, 128).T)
    m["up_w"] = f(inp["L0_ffn_up_w"])
    m["up_b_t"] = f(inp["L0_ffn_up_b"].reshape(NJ, 128).T)
    m["router_w"] = f(inp["L0_ffn_router_w"])
    m["rb_col"] = f(inp["L0_ffn_router_b"].reshape(8, 1))
    m["ncost_col"] = f((-_cost).reshape(8, 1)).astype(np.float32)
    m["shared_w1"] = f(inp["L0_ffn_shared_w1"])
    m["shared_b1_t"] = f(inp["L0_ffn_shared_b1"].reshape(8, 128).T)
    m["shared_w2"] = f(inp["L0_ffn_shared_w2"])
    m["shared_b2_row"] = f(inp["L0_ffn_shared_b2"].reshape(1, L))
    m["core_w"] = f(inp["L0_ffn_core_w"])
    m["core_b_t"] = f(inp["L0_ffn_core_b"].reshape(NJ, 128).T)
    for e in range(8):
        m[f"w1_{e}"] = f(inp[f"L0_ffn_e{e}_w1"])
        m[f"w2_{e}"] = f(inp[f"L0_ffn_e{e}_w2"])
        m[f"b1_{e}_t"] = f(inp[f"L0_ffn_e{e}_b1"].reshape(NCH[e], 128).T)
    m["b2all"] = f(np.stack([inp[f"L0_ffn_e{e}_b2"] for e in range(8)]))
    m["ident_bf"] = np.eye(128, dtype=ml_dtypes.bfloat16)
    m["ident8_f"] = np.eye(8, dtype=np.float32)
    m["ones_row_bf"] = np.ones((1, 128), dtype=ml_dtypes.bfloat16)
    m["ones_col_bf"] = np.ones((128, 1), dtype=ml_dtypes.bfloat16)
    sg = np.concatenate([-np.ones(8), [0.1], np.ones(8)]).astype(
        ml_dtypes.bfloat16).reshape(17, 1)
    m["sgn17_col"] = sg
    m["p01_col"] = np.full((1, 1), 0.1, dtype=ml_dtypes.bfloat16)
    return m


def kernel(**inputs):
    x = np.asarray(inputs["L0_attn_hidden_states"], dtype=np.float32)
    wmap = _stage_weights({k: np.asarray(v, dtype=np.float32)
                           for k, v in inputs.items()
                           if k != "L0_attn_hidden_states"})
    nc = build_nc()
    in_maps = []
    for c in range(8):
        b, half = c // 2, c % 2
        xtb_ = np.ascontiguousarray(x[b].T)
        own = xtb_[:, half * 1024:(half + 1) * 1024]
        oth = xtb_[:, (1 - half) * 1024:(2 - half) * 1024]
        im = dict(wmap)
        im["xt"] = np.ascontiguousarray(np.concatenate([own, oth], axis=1))
        in_maps.append(im)
    res = run_bass_kernel_spmd(nc, in_maps, core_ids=list(range(8)))
    y = np.empty((4, 2048, 2048), dtype=np.float32)
    for c in range(8):
        b, half = c // 2, c % 2
        y[b, half * 1024:(half + 1) * 1024, :] = res.results[c]["out"].T
    return y


# version 11 (final)
# speedup vs baseline: 1.0119x; 1.0119x over previous
import numpy as np
import ml_dtypes
import concourse.bass as bass
import concourse.mybir as mybir
from concourse.bass_utils import run_bass_kernel_spmd

F32 = mybir.dt.float32
BF16 = mybir.dt.bfloat16
AF = mybir.ActivationFunctionType
ALU = mybir.AluOpType
AX = mybir.AxisListType

J = 2048
T = 1024
S = 2048
NJ = 16
NT1 = 8
NS1 = 16
L = 1024
NL = 8
HE = [1024, 2048, 3072, 4096, 1024, 2048, 3072, 4096]
NCH = [h // 128 for h in HE]
CH_BASE = np.cumsum([0] + NCH).tolist()
NCH_TOT = sum(NCH)
SH_B1 = NCH_TOT
EPS_ATTN = 1e-6
EPS_LN = 1e-5


class Ctr:
    def __init__(self, handle):
        self.h = handle
        self.v = 0

    def add(self, n=1):
        self.v += n
        return self.v


def _groups():
    gs = []
    for e in range(8):
        hc0 = 0
        while hc0 < NCH[e]:
            n = min(8, NCH[e] - hc0)
            gs.append(dict(e=e, shared=False, hc0=hc0, nch=n, gb=CH_BASE[e]))
            hc0 += n
    gs.append(dict(e=None, shared=True, hc0=0, nch=8, gb=SH_B1))
    return gs


GROUPS = _groups()
N_PHANT = sum(g["nch"] for g in GROUPS if not g["shared"])
DEBUG_DUMPS = False
PHASE_LIMIT = 99
K_MODE = "full"


class _StopBuild(Exception):
    pass


def build_nc():
    nc = bass.Bass("TRN2", target_bir_lowering=False,
                   detect_race_conditions=False)

    def din(name, shape, dt=F32):
        return nc.dram_tensor(name, list(shape), dt, kind="ExternalInput")

    xt = din("xt", (J, S))
    wqt = din("wqt", (J, J))
    wkt = din("wkt", (J, 512))
    wvt = din("wvt", (J, 512))
    wot = din("wot", (J, J))
    bq_t = din("bq_t", (128, NJ))
    bk_row = din("bk_row", (1, 512))
    bv_row = din("bv_row", (1, 512))
    down_w = din("down_w", (J, L))
    down_b_t = din("down_b_t", (128, NL))
    up_w = din("up_w", (L, J))
    up_b_t = din("up_b_t", (128, NJ))
    router_w = din("router_w", (J, 8))
    rb_col = din("rb_col", (8, 1))
    ncost_col = din("ncost_col", (8, 1))
    shared_w1 = din("shared_w1", (L, L))
    shared_b1_t = din("shared_b1_t", (128, 8))
    shared_w2 = din("shared_w2", (L, L))
    shared_b2_row = din("shared_b2_row", (1, L))
    core_w = din("core_w", (J, J))
    core_b_t = din("core_b_t", (128, NJ))
    w1d = [din(f"w1_{e}", (L, HE[e])) for e in range(8)]
    w2d = [din(f"w2_{e}", (HE[e], L)) for e in range(8)]
    b1d = [din(f"b1_{e}_t", (128, NCH[e])) for e in range(8)]
    b2all = din("b2all", (8, L))
    ident_bf = din("ident_bf", (128, 128), BF16)
    ident8_f = din("ident8_f", (8, 8))
    ones_row_bf = din("ones_row_bf", (1, 128), BF16)
    ones_col_bf = din("ones_col_bf", (128, 1), BF16)
    sgn17_col = din("sgn17_col", (17, 1), BF16)
    p01_col = din("p01_col", (1, 1), BF16)

    out = nc.dram_tensor("out", [J, T], F32, kind="ExternalOutput")

    import contextlib
    stx = contextlib.ExitStack()

    def sb(name, shape, dt):
        return stx.enter_context(nc.sbuf_tensor(name, list(shape), dt))

    pb = [stx.enter_context(nc.psum_tensor(f"pb{i}", [128, 512], F32))
          for i in range(6)]
    pbx = [stx.enter_context(nc.psum_tensor(f"pbx{i}", [128, 512], F32))
           for i in range(2)]
    pbf = [t.bitcast(BF16)[:, 0:128] for t in pbx]

    arA = sb("arA", (128, 16384), F32)
    arB = sb("arB", (128, 8192), F32)
    arC = sb("arC", (128, 8192), F32)
    arE = sb("arE", (128, 4096), F32)
    arF = sb("arF", (128, 4224), F32)
    wsl = [sb(f"wsl{i}", (128, 2048), F32) for i in range(2)]
    stg = sb("stg", (128, 1024), F32)
    w2st = [sb(f"w2st{i}", (128, 1024), F32) for i in range(2)]
    tscr = sb("tscr", (128, 3072), F32)

    def bfv(t):
        return t[:].bitcast(BF16)

    xtb = lambda jc: bfv(arA)[:, jc * 2048:(jc + 1) * 2048]
    y1v = lambda jc: arA[:, jc * 1024:(jc + 1) * 1024]
    sqv = lambda h: bfv(arB)[:, h * 1024:(h + 1) * 1024]
    x2bv = lambda jc: bfv(arB)[:, jc * 1024:(jc + 1) * 1024]
    w2g = lambda i: bfv(arB)[:, i * 1024:(i + 1) * 1024]
    atv = lambda fc: bfv(arC)[:, fc * 1024:(fc + 1) * 1024]
    accv = lambda ti: arC[:, ti * 1024:(ti + 1) * 1024]
    gx2v = lambda jc: bfv(arC)[:, jc * 1024:(jc + 1) * 1024]
    skv = lambda s_: bfv(arE)[:, s_ * 512:(s_ + 1) * 512]
    gTv = lambda lc: bfv(arE)[:, lc * 1024:(lc + 1) * 1024]
    vaug = lambda s_: bfv(arF)[:, s_ * 520:s_ * 520 + 520]
    Uv = lambda i: bfv(arF)[:, i * 1024:(i + 1) * 1024]
    moev = lambda lc: bfv(arF)[:, lc * 1024:(lc + 1) * 1024]
    logit = arF[0:8, 0:1024]
    wslv = lambda i: bfv(wsl[i])
    upslv = lambda i: bfv(tscr)[:, i * 2048:(i + 1) * 2048]
    stgv = lambda i: stg[:, (i % 4) * 256:(i % 4) * 256 + 256]
    t1 = [tscr[:, 0:512], tscr[:, 512:1024]]
    t2 = [tscr[:, 1024:1536], tscr[:, 1536:2048]]
    t3 = [tscr[:, 2048:2560], tscr[:, 2560:3072]]
    r_m = stg[0:1, 0:512]
    r_t = stg[0:1, 512:1024]
    r_r = w2st[0][0:1, 0:512]
    r_mr = w2st[0][0:1, 512:1024]
    r_rb = w2st[1][0:1, 0:512].bitcast(BF16)[:, 0:512]
    r_mrb = w2st[1][0:1, 512:1024].bitcast(BF16)[:, 0:512]
    bkr = w2st[0][0:1, 0:512]
    bkrow = sb("bkrow", (1, 512), BF16)
    bkrb = bkrow[0:1, 0:512]
    b2f = w2st[0][0:8, :]
    b2sf = w2st[1][0:1, :]
    rows17 = stg[0:17, :].bitcast(BF16)[:, 0:1024]
    b2bf = stg[0:8, :].bitcast(BF16)[:, 1024:2048]
    waug = tscr[0:9, 2048:2560].bitcast(BF16)[:, 0:1024]

    tbx = [sb(f"tb_{i}", (128, 512), BF16) for i in range(2)]
    bqs = sb("bqs", (128, NJ), F32)
    dbs = sb("dbs", (128, NL), F32)
    ubs = sb("ubs", (128, NJ), F32)
    cbs = sb("cbs", (128, NJ), F32)
    bsum = sb("bsum", (128, NJ), F32)
    b1all = sb("b1all", (128, NCH_TOT + 8), F32)
    gb1all = sb("gb1all", (128, NCH_TOT), BF16)
    phg = sb("phg", (128, 64), BF16)
    rbc = sb("rbc", (8, 1), F32)
    ncc = sb("ncc", (8, 1), F32)
    epsc = sb("epsc", (128, 1), F32)
    idb = sb("idb", (128, 128), BF16)
    id8 = sb("id8", (8, 8), F32)
    onr = sb("onr", (1, 128), BF16)
    onc = sb("onc", (128, 1), BF16)
    sgn = sb("sgn", (17, 1), BF16)
    p01 = sb("p01", (1, 1), BF16)
    rws = sb("rws", (128, NJ * 8), F32)
    kvaug = sb("kvaug", (128, 4 * 130), BF16)
    wgt = sb("wgt", (128, NT1 * 8), F32)
    wbt = sb("wbt", (128, NT1 * 8), BF16)

    rt = [sb(f"rt{i}", (128, 8), F32) for i in range(4)]
    rc = [sb(f"rc{i}", (128, 1), F32) for i in range(6)]

    sems = {}
    for name in ["ldb", "stf", "gpc",
                 "aca", "acb", "dva", "dvb", "mma", "mmb", "mmc"]:
        sems[name] = Ctr(stx.enter_context(nc.semaphore(name)))
    ldb = sems["ldb"]
    stf, gpc = sems["stf"], sems["gpc"]
    aca, acb, dva, dvb = (sems[k] for k in ["aca", "acb", "dva", "dvb"])
    mma, mmb, mmc = sems["mma"], sems["mmb"], sems["mmc"]
    ldp = [Ctr(stx.enter_context(nc.semaphore(f"ldp{i}"))) for i in range(4)]
    dvs = Ctr(stx.enter_context(nc.semaphore("dvs")))
    acs = Ctr(stx.enter_context(nc.semaphore("acs")))
    gps = Ctr(stx.enter_context(nc.semaphore("gps")))

    def chain(eng, instr, ctr):
        instr.then_inc(ctr.h, 1)
        ctr.add(1)
        eng.wait_ge(ctr.h, ctr.v)
    ldq = [Ctr(stx.enter_context(nc.semaphore(f"ldq{i}"))) for i in range(2)]
    ldx2 = [Ctr(stx.enter_context(nc.semaphore(f"ldx{i}"))) for i in range(2)]

    try:
        with nc.Block("init") as blk:
            @blk.sync
            def _(sy):
                pairs = [(bq_t, bqs), (down_b_t, dbs), (up_b_t, ubs),
                         (core_b_t, cbs), (rb_col, rbc), (ncost_col, ncc),
                         (ident_bf, idb), (ident8_f, id8), (ones_row_bf, onr),
                         (ones_col_bf, onc), (sgn17_col, sgn), (p01_col, p01)]
                for t_, s_ in pairs:
                    sy.dma_start(s_[0:t_.shape[0], :], t_.ap()).then_inc(ldb.h, 16)
                    ldb.add(16)
                sy.dma_start(b1all[:, SH_B1:SH_B1 + 8],
                             shared_b1_t.ap()).then_inc(ldb.h, 16)
                ldb.add(16)
                for e in range(8):
                    sy.dma_start(b1all[:, CH_BASE[e]:CH_BASE[e] + NCH[e]],
                                 b1d[e].ap()).then_inc(ldb.h, 16)
                    ldb.add(16)
                for jc in range(NJ):
                    sy.dma_start(rws[:, jc * 8:(jc + 1) * 8],
                                 router_w.ap()[jc * 128:(jc + 1) * 128, :]
                                 ).then_inc(ldb.h, 16)
                    ldb.add(16)

            nb_all = ldb.v

            @blk.gpsimd
            def _(gp):
                dvs_b0 = dvs.v
                for jc in range(NJ):
                    if jc >= 1:
                        gp.wait_ge(dvs.h, dvs_b0 + 3 + 2 * jc)
                    gp.dma_start(w2st[0][:, :],
                                 xt.ap()[jc * 128:(jc + 1) * 128, 0:1024]
                                 ).then_inc(ldx2[0].h, 16)
                    ldx2[0].add(16)
                    gp.dma_start(w2st[1][:, :],
                                 xt.ap()[jc * 128:(jc + 1) * 128, 1024:2048]
                                 ).then_inc(ldx2[1].h, 16)
                    ldx2[1].add(16)

            @blk.scalar
            def _(sc):
                sc.wait_ge(ldb.h, nb_all)
                for e in range(8):
                    sc.activation(gb1all[:, CH_BASE[e]:CH_BASE[e] + NCH[e]],
                                  b1all[:, CH_BASE[e]:CH_BASE[e] + NCH[e]],
                                  AF.Gelu).then_inc(aca.h, 1)
                    aca.add(1)

            gb1_done = aca.v

            @blk.vector
            def _(v):
                v.memset(kvaug[:], 0.0)
                v.memset(epsc[:, 0:1], EPS_ATTN)
                chain(v, v.memset(rc[0][:, 0:1], 1.0), dvs)
                chain(v, v.reciprocal(rc[1][:, 0:1], rc[0][:, 0:1]), dvs)
                chain(v, v.reciprocal(rc[1][:, 0:1], rc[0][:, 0:1]), dvs)
                for jc in range(NJ):
                    v.wait_ge(ldx2[0].h, 16 * (jc + 1))
                    v.tensor_copy(xtb(jc)[:, 0:1024], w2st[0][:, :]
                                  ).then_inc(dvs.h, 1)
                    dvs.add(1)
                    v.wait_ge(ldx2[1].h, 16 * (jc + 1))
                    v.tensor_copy(xtb(jc)[:, 1024:2048], w2st[1][:, :]
                                  ).then_inc(dvs.h, 1)
                    dvs.add(1)
                v.wait_ge(dvs.h, dvs.v)
                v.nop().then_inc(dva.h, 1)
                dva.add(1)
            dvs_init_done = dvs.v

        dbg_ctr = Ctr(stx.enter_context(nc.semaphore("dbg"))) if DEBUG_DUMPS \
            else None

        def _ck(n):
            if PHASE_LIMIT <= n:
                raise _StopBuild()

        def dump(tag, ap, shape, dt):
            if not DEBUG_DUMPS:
                return
            d = nc.dram_tensor(f"dbg_{tag}", list(shape), dt, kind="ExternalOutput")
            with nc.Block(f"dump_{tag}") as bk2:
                @bk2.gpsimd
                def _(gp):
                    gp.dma_start(d.ap(), ap).then_inc(dbg_ctr.h, 16)
                    dbg_ctr.add(16)
                    gp.wait_ge(dbg_ctr.h, dbg_ctr.v)

        _ck(1)
        with nc.Block("kpass") as blk:
            ldb_b, aca_b, acb_b, mma_b, dva_b = ldb.v, aca.v, acb.v, mma.v, dva.v
            ldp_b = [s.v for s in ldp]

            @blk.sync
            def _(sy):
                sy.dma_start(bkr, bk_row.ap()).then_inc(ldb.h, 16)
                ldb.add(16)
                for jc in range(NJ):
                    if jc >= 2:
                        sy.wait_ge(aca.h, aca_b + jc)
                    sem = ldp[jc % 2]
                    sy.dma_start(stg[:, (jc % 2) * 512:(jc % 2) * 512 + 512],
                                 wkt.ap()[jc * 128:(jc + 1) * 128, :]
                                 ).then_inc(sem.h, 16)
                    sem.add(16)

            @blk.scalar
            def _(sc):
                sc.wait_ge(ldb.h, ldb_b + 16)
                sc.activation(bkrb, bkr, AF.Copy).then_inc(aca.h, 1)
                aca.add(1)
                for jc in range(NJ):
                    sc.wait_ge(ldp[jc % 2].h, ldp_b[jc % 2] + 16 * (jc // 2 + 1))
                    sc.activation(wslv(jc // 8)[:, (jc % 8) * 512:(jc % 8) * 512 + 512],
                                  stg[:, (jc % 2) * 512:(jc % 2) * 512 + 512],
                                  AF.Copy).then_inc(aca.h, 1)
                    aca.add(1)
                if K_MODE == "full":
                    for s_ in range(NS1):
                        sc.wait_ge(mma.h, mma_b + min(s_ + 2, NS1 + 1))
                        sc.activation(t1[s_ % 2], pb[s_ % 4][:], AF.Relu
                                      ).then_inc(acb.h, 1)
                        acb.add(1)
                        sc.wait_ge(dva.h, dva_b + 2 * s_ + 1)
                        sc.activation(t3[s_ % 2], t2[s_ % 2], AF.Exp
                                      ).then_inc(acb.h, 1)
                        acb.add(1)

            @blk.tensor
            def _(te):
                if K_MODE != "nomm":
                    te.wait_ge(aca.h, aca_b + NJ + 1)
                    for s_ in range(NS1):
                        if s_ >= 4:
                            te.wait_ge(dva.h, dva_b + 2 * (s_ - 4) + 2)
                            if K_MODE == "full":
                                te.wait_ge(acb.h, acb_b + 2 * (s_ - 4) + 1)
                        for jc in range(NJ):
                            te.matmul(pb[s_ % 4][:, 0:512],
                                      xtb(jc)[:, s_ * 128:(s_ + 1) * 128],
                                      wslv(jc // 8)[:, (jc % 8) * 512:(jc % 8) * 512 + 512],
                                      start=(jc == 0), stop=False)
                        te.matmul(pb[s_ % 4][:, 0:512], onr[0:1, 0:128],
                                  bkrb, start=False, stop=True
                                  ).then_inc(mma.h, 1)
                        mma.add(1)
                        if s_ == NS1 - 1:
                            te.drain().then_inc(mma.h, 1)
                            mma.add(1)

            @blk.vector
            def _(v):
                for s_ in range(NS1):
                    if K_MODE != "nomm":
                        v.wait_ge(mma.h, mma_b + min(s_ + 2, NS1 + 1))
                    if K_MODE == "full":
                        v.wait_ge(acb.h, acb_b + 2 * s_ + 1)
                        v.tensor_scalar_min(t2[s_ % 2], pb[s_ % 4][:], 0.0
                                            ).then_inc(dva.h, 1)
                        dva.add(1)
                        v.wait_ge(acb.h, acb_b + 2 * s_ + 2)
                        v.tensor_add(skv(s_), t1[s_ % 2], t3[s_ % 2]
                                     ).then_inc(dva.h, 1)
                        dva.add(1)
                    elif K_MODE == "noepi":
                        v.tensor_copy(skv(s_), pb[s_ % 2][:]).then_inc(dva.h, 2)
                        dva.add(2)
                    else:
                        v.tensor_copy(skv(s_), t1[s_ % 2]).then_inc(dva.h, 2)
                        dva.add(2)

        _ck(2)
        with nc.Block("vpass") as blk:
            ldb_b, aca_b, mma_b, mmb_b, dva_b = ldb.v, aca.v, mma.v, mmb.v, dva.v
            ldp_b = [s.v for s in ldp]

            @blk.sync
            def _(sy):
                sy.dma_start(bkr, bv_row.ap()).then_inc(ldb.h, 16)
                ldb.add(16)
                for jc in range(NJ):
                    if jc >= 2:
                        sy.wait_ge(aca.h, aca_b + jc)
                    sem = ldp[jc % 2]
                    sy.dma_start(stg[:, (jc % 2) * 512:(jc % 2) * 512 + 512],
                                 wvt.ap()[jc * 128:(jc + 1) * 128, :]
                                 ).then_inc(sem.h, 16)
                    sem.add(16)

            @blk.scalar
            def _(sc):
                sc.wait_ge(ldb.h, ldb_b + 16)
                sc.activation(bkrb, bkr, AF.Copy).then_inc(aca.h, 1)
                aca.add(1)
                for jc in range(NJ):
                    sc.wait_ge(ldp[jc % 2].h, ldp_b[jc % 2] + 16 * (jc // 2 + 1))
                    sc.activation(wslv(jc // 8)[:, (jc % 8) * 512:(jc % 8) * 512 + 512],
                                  stg[:, (jc % 2) * 512:(jc % 2) * 512 + 512],
                                  AF.Copy).then_inc(aca.h, 1)
                    aca.add(1)

            @blk.tensor
            def _(te):
                te.wait_ge(aca.h, aca_b + NJ + 1)
                for s_ in range(NS1 + 1):
                    if s_ < NS1:
                        if s_ >= 4:
                            te.wait_ge(dva.h, dva_b + (s_ - 4) + 1)
                        for jc in range(NJ):
                            te.matmul(pb[s_ % 4][:, 0:512],
                                      xtb(jc)[:, s_ * 128:(s_ + 1) * 128],
                                      wslv(jc // 8)[:, (jc % 8) * 512:(jc % 8) * 512 + 512],
                                      start=(jc == 0), stop=False)
                        te.matmul(pb[s_ % 4][:, 0:512], onr[0:1, 0:128],
                                  bkrb, start=False, stop=True).then_inc(mma.h, 1)
                        mma.add(1)
                        if s_ == NS1 - 1:
                            te.drain().then_inc(mma.h, 1)
                            mma.add(1)
                    if s_ >= 1:
                        sk_ = s_ - 1
                        te.wait_ge(dva.h, dva_b + sk_ + 1)
                        for g in range(4):
                            te.matmul(
                                pb[4 + g // 2][:, (g % 2) * 130:(g % 2) * 130 + 129],
                                skv(sk_)[:, g * 128:(g + 1) * 128],
                                vaug(sk_)[:, g * 130:g * 130 + 129],
                                start=(sk_ == 0 and g % 2 == 0),
                                stop=(sk_ == NS1 - 1),
                                skip_group_check=True)
                        if sk_ == NS1 - 1:
                            te.drain().then_inc(mmb.h, 1)
                        else:
                            te.nop().then_inc(mmb.h, 1)
                        mmb.add(1)

            @blk.vector
            def _(v):
                ones4d = bfv(arF)[:, 0:16 * 520].rearrange(
                    "p (s g c) -> p s g c", g=4, c=130)[:, :, :, 128:130]
                chain(v, v.memset(ones4d, 1.0), dvs)
                for s_ in range(NS1):
                    v.wait_ge(mma.h, mma_b + min(s_ + 2, NS1 + 1))
                    dst = vaug(s_).rearrange("p (g c) -> p g c", c=130)[:, :, 0:128]
                    src = pb[s_ % 4][:].rearrange("p (g c) -> p g c", c=128)
                    v.tensor_copy(dst, src).then_inc(dva.h, 1)
                    dva.add(1)
                v.wait_ge(mmb.h, mmb_b + NS1)
                for g in range(4):
                    cp = v.tensor_copy(
                        kvaug[:, g * 130:g * 130 + 129],
                        pb[4 + g // 2][:, (g % 2) * 130:(g % 2) * 130 + 129])
                    if g == 3:
                        cp.then_inc(dva.h, 1)
                dva.add(1)

        dump("sk", bfv(arE)[:, 0:8192], (128, 8192), BF16)
        dump("vaug", bfv(arF)[:, 0:8320], (128, 8320), BF16)
        dump("kvaug", kvaug[:, :], (128, 520), BF16)

        _ck(3)
        def lhs_stream_phase(name, wdram, nin, nsl, rhs_fn, epi_act, epi_dve,
                            extra_gp=None, extra_mm_rhs=None,
                            act_reads_psum=False):
            with nc.Block(name) as blk:
                bases = dict(aca=aca.v, acb=acb.v,
                             mma=mma.v, dva=dva.v, dvb=dvb.v,
                             ldp=[s.v for s in ldp], ldq=[s.v for s in ldq],
                             ldx2=[s.v for s in ldx2])

                @blk.sync
                def _(sy):
                    k = 0
                    for osl in range(nsl):
                        for ic in range(nin):
                            if k >= 4:
                                sy.wait_ge(aca.h, bases["aca"] + (k - 3))
                            sem = ldp[k % 4]
                            sy.dma_start(stgv(k),
                                         wdram.ap()[ic * 128:(ic + 1) * 128,
                                                    osl * 256:osl * 256 + 256]
                                         ).then_inc(sem.h, 16)
                            sem.add(16)
                            k += 1

                if extra_gp is not None:
                    extra_gp(blk, bases)

                @blk.scalar
                def _(sc):
                    np_ = 0
                    for osl in range(nsl + 1):
                        if osl < nsl:
                            if osl >= 2:
                                sc.wait_ge(mma.h, bases["mma"] + 4 * (osl - 1))
                            for ic in range(nin):
                                sc.wait_ge(ldp[np_ % 4].h,
                                           bases["ldp"][np_ % 4]
                                           + 16 * (np_ // 4 + 1))
                                sc.activation(
                                    wslv(osl % 2)[:, ic * 256:(ic + 1) * 256],
                                    stgv(np_), AF.Copy).then_inc(aca.h, 1)
                                aca.add(1)
                                np_ += 1
                        if osl >= 1:
                            for q in range(4):
                                i = (osl - 1) * 4 + q
                                epi_act(sc, i, bases)

                @blk.tensor
                def _(te):
                    i = 0
                    for osl in range(nsl):
                        te.wait_ge(aca.h, bases["aca"] + nin * (osl + 1))
                        for k2 in range(2):
                            for th in range(2):
                                if i >= 4:
                                    te.wait_ge(dva.h, bases["dva"] + 2 * (i - 4) + 2)
                                    if act_reads_psum:
                                        te.wait_ge(acb.h,
                                                   bases["acb"] + 2 * (i - 4) + 1)
                                for ic in range(nin):
                                    mm = te.matmul(
                                        pb[i % 4][:, 0:512],
                                        wslv(osl % 2)[:, ic * 256 + k2 * 128:
                                                      ic * 256 + k2 * 128 + 128],
                                        rhs_fn(ic, th),
                                        start=(ic == 0),
                                        stop=(ic == nin - 1))
                                    if ic == nin - 1:
                                        mm.then_inc(mma.h, 1)
                                mma.add(1)
                                i += 1
                        if osl == nsl - 1:
                            te.drain().then_inc(mma.h, 1)
                            mma.add(1)

                @blk.vector
                def _(v):
                    for i in range(4 * nsl):
                        epi_dve(v, i, bases)
            return bases

        def q_epi_act(sc, i, bases):
            c = i // 2
            sc.wait_ge(mma.h, bases["mma"] + min(i + 2, 4 * 8 + 1))
            sc.activation(t1[i % 2], pb[i % 4][:], AF.Relu,
                          bias=bqs[:, c:c + 1]).then_inc(acb.h, 1)
            acb.add(1)
            sc.wait_ge(dva.h, bases["dva"] + 2 * i + 1)
            sc.activation(t3[i % 2], t2[i % 2], AF.Exp).then_inc(acb.h, 1)
            acb.add(1)

        def q_epi_dve(v, i, bases):
            c, th = i // 2, i % 2
            v.wait_ge(mma.h, bases["mma"] + min(i + 2, 4 * 8 + 1))
            v.wait_ge(acb.h, bases["acb"] + 2 * i + 1)
            v.tensor_scalar(t2[i % 2], pb[i % 4][:], bqs[:, c:c + 1], 0.0,
                            ALU.add, ALU.min).then_inc(dva.h, 1)
            dva.add(1)
            v.wait_ge(acb.h, bases["acb"] + 2 * i + 2)
            v.tensor_add(sqv(c)[:, th * 512:th * 512 + 512],
                         t1[i % 2], t3[i % 2]).then_inc(dva.h, 1)
            dva.add(1)

        lhs_stream_phase("qproj", wqt, NJ, 8,
                         lambda jc, th: xtb(jc)[:, th * 512:th * 512 + 512],
                         q_epi_act, q_epi_dve, act_reads_psum=True)

        dump("sqT", bfv(arB)[:, 0:16384], (128, 16384), BF16)

        _ck(4)
        with nc.Block("qkv") as blk:
            mma_b, mmb_b, dvb_b, acb_b = mma.v, mmb.v, dvb.v, acb.v

            @blk.tensor
            def _(te):
                for i in range(128):
                    h, ti = i // NT1, i % NT1
                    if i >= 4:
                        te.wait_ge(acb.h, acb_b + (i - 4) + 1)
                    te.matmul(pb[i % 4][:, 0:129],
                              sqv(h)[:, ti * 128:(ti + 1) * 128],
                              kvaug[:, (h // 4) * 130:(h // 4) * 130 + 129],
                              start=True, stop=True).then_inc(mma.h, 1)
                    mma.add(1)
                    if i >= 2:
                        k = i - 2
                        te.wait_ge(acb.h, acb_b + k + 1)
                        if k >= 2:
                            te.wait_ge(dvb.h, dvb_b + k - 1)
                        te.transpose(pbf[k % 2][:], tbx[k % 2][:, 0:128],
                                     idb[:]).then_inc(mmb.h, 1)
                        mmb.add(1)
                te.drain().then_inc(mma.h, 1)
                mma.add(1)
                for k in (126, 127):
                    te.wait_ge(acb.h, acb_b + k + 1)
                    te.wait_ge(dvb.h, dvb_b + k - 1)
                    te.transpose(pbf[k % 2][:], tbx[k % 2][:, 0:128], idb[:]
                                 ).then_inc(mmb.h, 1)
                    mmb.add(1)
                te.drain().then_inc(mmb.h, 1)
                mmb.add(1)

            acs_b, dva_b = acs.v, dva.v

            @blk.scalar
            def _(sc):
                for i in range(129):
                    if i < 128:
                        sc.wait_ge(mma.h, mma_b + min(i + 2, 129))
                        if i >= 2:
                            sc.wait_ge(dva.h, dva_b + i - 1)
                        sc.activation(rc[4 + i % 2][:, 0:1],
                                      pb[i % 4][:, 128:129], AF.Copy,
                                      bias=EPS_ATTN).then_inc(acs.h, 1)
                        acs.add(1)
                    if i >= 1:
                        k = i - 1
                        if k >= 2:
                            sc.wait_ge(mmb.h, mmb_b + k - 1)
                        sc.wait_ge(dva.h, dva_b + k + 1)
                        sc.activation(tbx[k % 2][:, 0:128],
                                      pb[k % 4][:, 0:128],
                                      AF.Copy, scale=rc[2 + k % 2][:, 0:1]
                                      ).then_inc(acb.h, 1)
                        acb.add(1)

            @blk.vector
            def _(v):
                for i in range(129):
                    if i < 128:
                        v.wait_ge(acs.h, acs_b + i + 1)
                        if i >= 2:
                            v.wait_ge(acb.h, acb_b + i - 1)
                        v.reciprocal(rc[2 + i % 2][:, 0:1],
                                     rc[4 + i % 2][:, 0:1]).then_inc(dva.h, 1)
                        dva.add(1)
                    if i >= 1:
                        k = i - 1
                        h, ti = k // NT1, k % NT1
                        v.wait_ge(mmb.h, mmb_b + min(k + 2, 129))
                        v.tensor_copy(atv(h)[:, ti * 128:(ti + 1) * 128],
                                      pbf[k % 2][:]).then_inc(dvb.h, 1)
                        dvb.add(1)

        dump("attnT", bfv(arC)[:, 0:16384], (128, 16384), BF16)
        dump("qdbg", stg[:, 0:512], (128, 512), F32)

        _ck(5)
        def wo_extra_gp(blk, bases):
            @blk.gpsimd
            def _(gp):
                for oc in range(NJ):
                    if oc < 2:
                        gp.wait_ge(dvs.h, dvs_init_done)
                    else:
                        gp.wait_ge(dva.h, bases["dva"] + 2 * (2 * (oc - 2) + 1) + 2)
                    sem = ldx2[oc % 2]
                    gp.dma_start(w2st[oc % 2][:, :],
                                 xt.ap()[oc * 128:(oc + 1) * 128, 0:1024]
                                 ).then_inc(sem.h, 16)
                    sem.add(16)

        def wo_epi_act(sc, i, bases):
            pass

        def wo_epi_dve(v, i, bases):
            oc, th = i // 2, i % 2
            v.wait_ge(mma.h, bases["mma"] + min(i + 2, 4 * 8 + 1))
            v.wait_ge(ldx2[oc % 2].h, bases["ldx2"][oc % 2] + 16 * (oc // 2 + 1))
            v.tensor_add(y1v(oc)[:, th * 512:th * 512 + 512], pb[i % 4][:],
                         w2st[oc % 2][:, th * 512:th * 512 + 512]
                         ).then_inc(dva.h, 2)
            dva.add(2)

        lhs_stream_phase("wo", wot, NJ, 8,
                         lambda fc, th: atv(fc)[:, th * 512:th * 512 + 512],
                         wo_epi_act, wo_epi_dve, extra_gp=wo_extra_gp)

        def ln_phase(name, also_bf16_copy, final_store, ready_off):
            with nc.Block(name) as blk:
                mma_b, mmb_b, dva_b, dvb_b, aca_b = mma.v, mmb.v, dva.v, dvb.v, aca.v
                stf_b = stf.v

                @blk.vector
                def _(v):
                    for th in range(2):
                        v.wait_ge(mma.h, mma_b + 16 * (th + 1))
                        chain(v, v.tensor_scalar(r_m, pb[4 + th][0:1, 0:512],
                                                 1.0 / J, None, ALU.mult), dvs)
                        chain(v, v.tensor_tensor(r_t, r_m, r_m, ALU.mult), dvs)
                        chain(v, v.scalar_tensor_tensor(r_t, pbx[th][0:1, 0:512],
                                                        1.0 / J, r_t, ALU.mult,
                                                        ALU.subtract), dvs)
                        v.tensor_scalar(r_t, r_t, EPS_LN, None, ALU.add
                                        ).then_inc(dvb.h, 1)
                        dvb.add(1)
                        v.wait_ge(aca.h, aca_b + 17 * (th + 1))
                        chain(v, v.reciprocal(r_r, r_mr), dvs)
                        chain(v, v.tensor_tensor(r_mr, r_m, r_r, ALU.mult), dvs)
                        v.tensor_copy(r_rb, r_r)
                        v.tensor_copy(r_mrb, r_mr).then_inc(dvb.h, 1)
                        dvb.add(1)
                        v.wait_ge(mmb.h, mmb_b + 2 * (th + 1))
                        for oc in range(NJ):
                            sl = y1v(oc)[:, th * 512:th * 512 + 512]
                            chain(v, v.tensor_tensor(t2[oc % 2], sl, pb[0][:],
                                                     ALU.mult), dvs)
                            chain(v, v.tensor_tensor(sl, t2[oc % 2], pb[1][:],
                                                     ALU.subtract), dvs)
                            if also_bf16_copy:
                                v.tensor_copy(x2bv(oc)[:, th * 512:th * 512 + 512],
                                              sl)
                            v.nop().then_inc(dvb.h, 1)
                            dvb.add(1)

                @blk.scalar
                def _(sc):
                    i = 0
                    for th in range(2):
                        for oc in range(NJ):
                            if i >= 2:
                                sc.wait_ge(mma.h, mma_b + (i - 2) + 1)
                            sc.wait_ge(dva.h, dva_b + ready_off(th, oc))
                            sl = y1v(oc)[:, th * 512:th * 512 + 512]
                            sc.activation(tbx[i % 2][:], sl, AF.Copy)
                            sc.activation(t1[i % 2].bitcast(BF16)[:, 0:512],
                                          sl, AF.Square).then_inc(aca.h, 1)
                            aca.add(1)
                            i += 1
                        sc.wait_ge(dvb.h, dvb_b + 18 * th + 1)
                        sc.activation(r_mr, r_t, AF.Sqrt
                                      ).then_inc(aca.h, 1)
                        aca.add(1)

                @blk.tensor
                def _(te):
                    i = 0
                    for th in range(2):
                        for oc in range(NJ):
                            te.wait_ge(aca.h, aca_b + i + 1 + th)
                            te.matmul(pb[4 + th][0:1, 0:512], onc[:, 0:1],
                                      tbx[i % 2][:], start=(oc == 0),
                                      stop=(oc == NJ - 1))
                            mm = te.matmul(pbx[th][0:1, 0:512], onc[:, 0:1],
                                           t1[i % 2].bitcast(BF16)[:, 0:512],
                                           start=(oc == 0), stop=(oc == NJ - 1))
                            if oc == NJ - 1:
                                te.drain().then_inc(mma.h, 1)
                            else:
                                mm.then_inc(mma.h, 1)
                            mma.add(1)
                            i += 1
                    for th in range(2):
                        te.wait_ge(dvb.h, dvb_b + 18 * th + 2)
                        te.matmul(pb[0][:, 0:512], onr[0:1, 0:128],
                                  r_rb[0:1, 0:512], start=True, stop=True)
                        te.drain().then_inc(mmb.h, 1)
                        mmb.add(1)
                        te.matmul(pb[1][:, 0:512], onr[0:1, 0:128],
                                  r_mrb[0:1, 0:512], start=True, stop=True)
                        te.drain().then_inc(mmb.h, 1)
                        mmb.add(1)

                if final_store:
                    @blk.sync
                    def _(sy):
                        for oc in range(NJ):
                            sy.wait_ge(dvb.h, dvb_b + 18 + 2 + (oc + 1))
                            sy.dma_start(out.ap()[oc * 128:(oc + 1) * 128, :],
                                         y1v(oc)).then_inc(stf.h, 16)
                            stf.add(16)
                        sy.wait_ge(stf.h, stf_b + 16 * NJ)

        _ck(6)
        dump("y1", arA[:, :], (128, 16384), F32)
        ln_phase("ln1", also_bf16_copy=True, final_store=False,
                 ready_off=lambda th, oc: -62 + 4 * oc + 2 * th)
        dump("x2", arA[:, :], (128, 16384), F32)

        _ck(7)
        with nc.Block("routerdown") as blk:
            RD = dict(mma=mma.v, mmb=mmb.v, mmc=mmc.v, dva=dva.v, dvb=dvb.v,
                      aca=aca.v, acb=acb.v, acs=acs.v,
                      ldp=[s.v for s in ldp])

            @blk.sync
            def _(sy):
                k = 0
                for osl in range(4):
                    for ic in range(NJ):
                        if k >= 4:
                            sy.wait_ge(aca.h, RD["aca"] + (k - 3))
                        sem = ldp[k % 4]
                        sy.dma_start(stgv(k),
                                     down_w.ap()[ic * 128:(ic + 1) * 128,
                                                 osl * 256:osl * 256 + 256]
                                     ).then_inc(sem.h, 16)
                        sem.add(16)
                        k += 1

            @blk.tensor
            def _(te):
                for th in range(2):
                    te.wait_ge(dvb.h, RD["dvb"] - 36 + 18 * th + 2)
                    te.wait_ge(dvb.h, RD["dvb"] - 18 * (1 - th))
                    for jc in range(NJ):
                        te.matmul(pb[4 + th][0:8, 0:512],
                                  rws[:, jc * 8:(jc + 1) * 8],
                                  y1v(jc)[:, th * 512:th * 512 + 512],
                                  start=(jc == 0), stop=(jc == NJ - 1))
                    te.drain().then_inc(mma.h, 1)
                    mma.add(1)
                te.wait_ge(dva.h, RD["dva"] + 1)
                for ti in range(NT1):
                    te.transpose(pb[2][:, ti * 8:ti * 8 + 8],
                                 logit[0:8, ti * 128:(ti + 1) * 128],
                                 id8[0:8, 0:8])
                te.drain().then_inc(mma.h, 1)
                mma.add(1)
                te.wait_ge(dvb.h, RD["dvb"])
                i = 0
                dbank = [0, 1, 3]
                for osl in range(4):
                    te.wait_ge(aca.h, RD["aca"] + NJ * (osl + 1))
                    for k2 in range(2):
                        for th in range(2):
                            if i >= 3:
                                te.wait_ge(acb.h, RD["acb"] + (i - 3) + 1)
                            for ic in range(NJ):
                                mm = te.matmul(
                                    pb[dbank[i % 3]][:, 0:512],
                                    wslv(osl % 2)[:, ic * 256 + k2 * 128:
                                                  ic * 256 + k2 * 128 + 128],
                                    x2bv(ic)[:, th * 512:th * 512 + 512],
                                    start=(ic == 0), stop=(ic == NJ - 1))
                                if ic == NJ - 1:
                                    mm.then_inc(mma.h, 1)
                            mma.add(1)
                            i += 1
                    if osl == 3:
                        te.drain().then_inc(mma.h, 1)
                        mma.add(1)

            @blk.scalar
            def _(sc):
                np_ = 0
                for osl in range(5):
                    if osl < 4:
                        if osl >= 2:
                            sc.wait_ge(mma.h, RD["mma"] + 3 + 4 * (osl - 1))
                        for ic in range(NJ):
                            sc.wait_ge(ldp[np_ % 4].h,
                                       RD["ldp"][np_ % 4]
                                       + 16 * (np_ // 4 + 1))
                            sc.activation(
                                wslv(osl % 2)[:, ic * 256:(ic + 1) * 256],
                                stgv(np_), AF.Copy).then_inc(aca.h, 1)
                            aca.add(1)
                            np_ += 1
                    if osl >= 1:
                        for q in range(4):
                            i = (osl - 1) * 4 + q
                            lc, th = i // 2, i % 2
                            sc.wait_ge(mma.h, RD["mma"] + 3 + min(i + 2, 17))
                            sc.activation(gTv(lc)[:, th * 512:th * 512 + 512],
                                          pb[[0, 1, 3][i % 3]][:], AF.Gelu,
                                          bias=dbs[:, lc:lc + 1]
                                          ).then_inc(acb.h, 1)
                            acb.add(1)
                for ti in range(NT1):
                    sc.wait_ge(dvb.h, RD["dvb"] + 3 * ti + 1)
                    sc.activation(rt[1][:], rt[0][:], AF.Exp, bias=rc[1][:, 0:1]
                                  ).then_inc(acs.h, 1)
                    acs.add(1)
                    sc.wait_ge(dvb.h, RD["dvb"] + 3 * ti + 2)
                    sc.activation(rc[3][:, 0:1], rc[3][:, 0:1], AF.Sigmoid)
                    sc.activation(rc[4][:, 0:1], rc[4][:, 0:1], AF.Sigmoid
                                  ).then_inc(acs.h, 1)
                    acs.add(1)

            @blk.vector
            def _(v):
                for th in range(2):
                    v.wait_ge(mma.h, RD["mma"] + th + 1)
                    ts = v.tensor_scalar(logit[0:8, th * 512:th * 512 + 512],
                                         pb[4 + th][0:8, 0:512], rbc[0:8, 0:1],
                                         ncc[0:8, 0:1], ALU.add, ALU.add)
                    if th == 1:
                        ts.then_inc(dva.h, 1)
                dva.add(1)
                v.wait_ge(mma.h, RD["mma"] + 3)
                for ti in range(NT1):
                    lg = rt[0]
                    chain(v, v.tensor_copy(lg[:], pb[2][:, ti * 8:ti * 8 + 8]),
                          dvs)
                    chain(v, v.reduce_max(rc[0][:, 0:1], lg[:], axis=AX.X), dvs)
                    v.tensor_scalar(rc[1][:, 0:1], rc[0][:, 0:1], -1.0, None,
                                    ALU.mult).then_inc(dvb.h, 1)
                    dvb.add(1)
                    v.wait_ge(acs.h, RD["acs"] + 2 * ti + 1)
                    chain(v, v.reduce_sum(rc[2][:, 0:1], rt[1][:], axis=AX.X),
                          dvs)
                    chain(v, v.reciprocal(rc[2][:, 0:1], rc[2][:, 0:1]), dvs)
                    chain(v, v.tensor_scalar(rt[1][:], rt[1][:], rc[2][:, 0:1],
                                             None, ALU.mult), dvs)
                    chain(v, v.tensor_single_scalar(rt[2][:], lg[:],
                                                    rc[0][:, 0:1], ALU.is_ge),
                          dvs)
                    chain(v, v.scalar_tensor_tensor(rt[3][:], rt[2][:], -1e4,
                                                    lg[:], ALU.mult, ALU.add),
                          dvs)
                    chain(v, v.reduce_max(rc[3][:, 0:1], rt[3][:], axis=AX.X),
                          dvs)
                    chain(v, v.tensor_single_scalar(rt[3][:], rt[3][:],
                                                    rc[3][:, 0:1], ALU.is_ge),
                          dvs)
                    chain(v, v.reduce_max(rc[4][:, 0:1], rt[1][:], axis=AX.X),
                          dvs)
                    chain(v, v.scalar_tensor_tensor(rt[0][:], rt[2][:], -2.0,
                                                    rt[1][:], ALU.mult,
                                                    ALU.add), dvs)
                    chain(v, v.reduce_max(rc[5][:, 0:1], rt[0][:], axis=AX.X),
                          dvs)
                    chain(v, v.tensor_tensor(rc[3][:, 0:1], rc[4][:, 0:1],
                                             rc[5][:, 0:1], ALU.subtract), dvs)
                    v.tensor_scalar(rc[4][:, 0:1], rc[3][:, 0:1], -1.0, None,
                                    ALU.mult).then_inc(dvb.h, 1)
                    dvb.add(1)
                    v.wait_ge(acs.h, RD["acs"] + 2 * ti + 2)
                    chain(v, v.tensor_scalar(rt[2][:], rt[2][:], rc[3][:, 0:1],
                                             None, ALU.mult), dvs)
                    chain(v, v.scalar_tensor_tensor(wgt[:, ti * 8:(ti + 1) * 8],
                                                    rt[3][:], rc[4][:, 0:1],
                                                    rt[2][:], ALU.mult,
                                                    ALU.add), dvs)
                    v.tensor_copy(wbt[:, ti * 8:(ti + 1) * 8],
                                  wgt[:, ti * 8:(ti + 1) * 8]).then_inc(dvb.h, 1)
                    dvb.add(1)

        _ck(9)
        with nc.Block("experts") as blk:
            F2 = dict(gpc=gpc.v, aca=aca.v, acb=acb.v,
                      mma=mma.v, mmb=mmb.v, mmc=mmc.v, dva=dva.v, dvb=dvb.v,
                      ldp=[s.v for s in ldp], ldq=[s.v for s in ldq])
            eb1 = [pb[0], pb[1], pbx[0]]
            eb2 = [pb[2], pb[3], pbx[1]]
            sch = []
            pz = 0
            pm = 0
            psl = 0
            pw2 = 0
            for gi, g in enumerate(GROUPS):
                nsl = g["nch"] // 2
                sch.append(dict(g=g, gi=gi, nsl=nsl, z0=pz, m0=pm, sl0=psl,
                                w0=pw2, zd0=pz + gi, md0=pm + gi))
                pz += 2 * g["nch"]
                pm += 16
                psl += nsl
                pw2 += g["nch"]
            TOT_Z, TOT_M, TOT_SL, TOT_W2 = pz, pm, psl, pw2
            slice_end_mma = []
            for e_ in sch:
                for s_ in range(e_["nsl"]):
                    c = e_["zd0"] + 4 * (s_ + 1)
                    if s_ == e_["nsl"] - 1:
                        c += 1
                    slice_end_mma.append(c)

            @blk.sync
            def _(sy):
                npiece = 0
                for e_ in sch:
                    g = e_["g"]
                    wsrc = shared_w1 if g["shared"] else w1d[g["e"]]
                    w2src = shared_w2 if g["shared"] else w2d[g["e"]]
                    pig = 0
                    for s_ in range(e_["nsl"]):
                        col0 = (g["hc0"] // 2 + s_) * 256
                        for lc in range(8):
                            if npiece >= 4:
                                sy.wait_ge(gpc.h, F2["gpc"] + (npiece - 3))
                            sem = ldp[npiece % 4]
                            sy.dma_start(stgv(npiece),
                                         wsrc.ap()[lc * 128:(lc + 1) * 128,
                                                   col0:col0 + 256]
                                         ).then_inc(sem.h, 16)
                            sem.add(16)
                            npiece += 1
                            pig += 1
                            if pig % 4 == 0:
                                hcl = pig // 4 - 1
                                kc = e_["w0"] + hcl
                                if kc >= 2:
                                    sy.wait_ge(aca.h, F2["aca"] + (kc - 1))
                                r0 = (g["hc0"] + hcl) * 128
                                sem = ldq[kc % 2]
                                sy.dma_start(w2st[kc % 2][:, :],
                                             w2src.ap()[r0:r0 + 128, :]
                                             ).then_inc(sem.h, 16)
                                sem.add(16)

            @blk.gpsimd
            def _(gp):
                npiece = 0
                for e_ in sch:
                    for s_ in range(e_["nsl"]):
                        sg = e_["sl0"] + s_
                        if sg >= 2:
                            gp.wait_ge(mma.h,
                                       F2["mma"] + slice_end_mma[sg - 2])
                        for lc in range(8):
                            gp.wait_ge(ldp[npiece % 4].h,
                                       F2["ldp"][npiece % 4]
                                       + 16 * (npiece // 4 + 1))
                            gp.tensor_copy(
                                wslv(sg % 2)[:, lc * 256:(lc + 1) * 256],
                                stgv(npiece)).then_inc(gpc.h, 1)
                            gpc.add(1)
                            npiece += 1

            @blk.scalar
            def _(sc):
                for e_ in sch:
                    g = e_["g"]
                    sc.wait_ge(mmc.h, F2["mmc"] + e_["gi"])
                    for zt in range(2 * g["nch"]):
                        hcl, th = zt // 2, zt % 2
                        ztg = e_["z0"] + zt
                        bcol = g["gb"] + g["hc0"] + hcl
                        sc.wait_ge(mma.h, F2["mma"] + e_["zd0"]
                                   + min(zt + 2, 2 * g["nch"] + 1))
                        sc.activation(Uv(hcl)[:, th * 512:th * 512 + 512],
                                      eb1[ztg % 3][:], AF.Gelu,
                                      bias=b1all[:, bcol:bcol + 1]
                                      ).then_inc(acb.h, 1)
                        acb.add(1)
                        if zt < g["nch"]:
                            kc = e_["w0"] + zt
                            sc.wait_ge(ldq[kc % 2].h,
                                       F2["ldq"][kc % 2] + 16 * (kc // 2 + 1))
                            sc.activation(w2g(zt)[:, :], w2st[kc % 2][:, :],
                                          AF.Copy).then_inc(aca.h, 1)
                            aca.add(1)

            @blk.tensor
            def _(te):
                first_ph = True
                first_tile = True
                for e_ in sch:
                    g = e_["g"]
                    for s_ in range(e_["nsl"]):
                        sg = e_["sl0"] + s_
                        te.wait_ge(gpc.h, F2["gpc"] + 8 * (sg + 1))
                        for k2 in range(2):
                            for th in range(2):
                                zl = (s_ * 2 + k2) * 2 + th
                                ztg = e_["z0"] + zl
                                if first_tile:
                                    te.wait_ge(dvb.h, F2["dvb"])
                                    first_tile = False
                                if ztg >= 3:
                                    te.wait_ge(acb.h, F2["acb"] + ztg - 2)
                                for lc in range(8):
                                    mm = te.matmul(
                                        eb1[ztg % 3][:, 0:512],
                                        wslv(sg % 2)[:, lc * 256 + k2 * 128:
                                                     lc * 256 + k2 * 128 + 128],
                                        gTv(lc)[:, th * 512:th * 512 + 512],
                                        start=(lc == 0), stop=(lc == 7))
                                    if lc == 7:
                                        mm.then_inc(mma.h, 1)
                                mma.add(1)
                                if zl == 2 * g["nch"] - 1:
                                    te.drain().then_inc(mma.h, 1)
                                    mma.add(1)
                    te.wait_ge(acb.h, F2["acb"] + e_["z0"] + 2 * g["nch"])
                    te.wait_ge(aca.h, F2["aca"] + e_["w0"] + g["nch"])
                    for ti in range(NT1):
                        for lh in range(2):
                            ml = ti * 2 + lh
                            mtg = e_["m0"] + ml
                            if mtg >= 3:
                                te.wait_ge(dva.h, F2["dva"] + mtg - 2)
                            for hcl in range(g["nch"]):
                                mm = te.matmul(eb2[mtg % 3][:, 0:512],
                                               Uv(hcl)[:, ti * 128:(ti + 1) * 128],
                                               w2g(hcl)[:, lh * 512:lh * 512 + 512],
                                               start=(hcl == 0),
                                               stop=(hcl == g["nch"] - 1))
                                if hcl == g["nch"] - 1:
                                    mm.then_inc(mmb.h, 1)
                            mmb.add(1)
                            if ml == 15:
                                te.drain().then_inc(mmb.h, 1)
                                mmb.add(1)
                    if not g["shared"]:
                        nb = sum(1 for x in sch[:e_["gi"] + 1]
                                 if not x["g"]["shared"])
                        te.wait_ge(dvb.h, F2["dvb"] + nb)
                        for hcl in range(g["nch"]):
                            hcg = g["gb"] + g["hc0"] + hcl
                            last = (hcg == N_PHANT - 1)
                            for lh in range(2):
                                mm = te.matmul(pb[4 + lh][0:8, 0:512],
                                               phg[:, hcl * 8:hcl * 8 + 8],
                                               w2g(hcl)[:, lh * 512:lh * 512 + 512],
                                               start=first_ph, stop=last,
                                               skip_group_check=True)
                                if hcl == g["nch"] - 1 and lh == 1:
                                    mm.then_inc(mmc.h, 1)
                            first_ph = False
                    else:
                        te.nop().then_inc(mmc.h, 1)
                    mmc.add(1)

            @blk.vector
            def _(v):
                ph3 = phg[:].rearrange("p (c k) -> p c k", k=8)
                for e_ in sch:
                    g = e_["g"]
                    if not g["shared"]:
                        v.wait_ge(mmc.h, F2["mmc"] + e_["gi"])
                        chain(v, v.memset(phg[:], 0.0), dvs)
                        b0 = g["gb"] + g["hc0"]
                        v.tensor_copy(ph3[:, 0:g["nch"], g["e"]:g["e"] + 1],
                                      gb1all[:, b0:b0 + g["nch"]].unsqueeze(2)
                                      ).then_inc(dvb.h, 1)
                        dvb.add(1)
                    for ti in range(NT1):
                        for lh in range(2):
                            ml = ti * 2 + lh
                            mtg = e_["m0"] + ml
                            v.wait_ge(mmb.h, F2["mmb"] + e_["md0"]
                                      + min(ml + 2, 17))
                            dst = accv(ti)[:, lh * 512:lh * 512 + 512]
                            if g["shared"]:
                                fl = v.scalar_tensor_tensor(
                                    dst, eb2[mtg % 3][:],
                                    0.1, dst, ALU.mult, ALU.add)
                            elif g["e"] == 0 and g["hc0"] == 0:
                                fl = v.tensor_scalar(dst, eb2[mtg % 3][:],
                                                     wgt[:, ti * 8 + g["e"]:
                                                         ti * 8 + g["e"] + 1],
                                                     None, ALU.mult)
                            else:
                                fl = v.scalar_tensor_tensor(
                                    dst, eb2[mtg % 3][:],
                                    wgt[:, ti * 8 + g["e"]:
                                        ti * 8 + g["e"] + 1],
                                    dst, ALU.mult, ALU.add)
                            fl.then_inc(dva.h, 1)
                            dva.add(1)

        _ck(10)
        with nc.Block("corr") as blk:
            mma_b, mmb_b, dva_b, dvb_b, aca_b = mma.v, mmb.v, dva.v, dvb.v, aca.v
            ldb_b = ldb.v
            cb3 = [pb[0], pb[1], pb[2]]
            mmc_b = mmc.v

            @blk.sync
            def _(sy):
                sy.dma_start(b2f, b2all.ap()).then_inc(ldb.h, 16)
                ldb.add(16)
                sy.dma_start(b2sf, shared_b2_row.ap()).then_inc(ldb.h, 16)
                ldb.add(16)

            @blk.vector
            def _(v):
                v.wait_ge(ldb.h, ldb_b + 32)
                v.tensor_copy(b2bf, b2f)
                v.tensor_copy(tbx[0][0:1, 0:512], b2sf[:, 0:512])
                v.tensor_copy(tbx[1][0:1, 0:512], b2sf[:, 512:1024])
                v.tensor_scalar(rows17[0:8, 0:512],
                                pb[4][0:8, 0:512], -1.0, None, ALU.mult)
                v.tensor_scalar(rows17[0:8, 512:1024],
                                pb[5][0:8, 0:512], -1.0, None, ALU.mult
                                ).then_inc(dva.h, 1)
                dva.add(1)
                for ti in range(NT1):
                    v.wait_ge(mmc.h, mmc_b + ti + 1)
                    v.tensor_copy(waug[0:8, ti * 128:(ti + 1) * 128],
                                  pbf[ti % 2][0:8, 0:128]).then_inc(dvb.h, 1)
                    dvb.add(1)
                for ti in range(NT1):
                    for lh in range(2):
                        i = ti * 2 + lh
                        v.wait_ge(mmb.h, mmb_b + min(i + 2, 17))
                        dst = accv(ti)[:, lh * 512:lh * 512 + 512]
                        v.tensor_tensor(dst, dst, cb3[i % 3][:], ALU.add
                                        ).then_inc(dvb.h, 1)
                        dvb.add(1)

            @blk.tensor
            def _(te):
                te.wait_ge(dva.h, dva_b + 1)
                for lh in range(2):
                    te.matmul(pb[2 + lh][0:1, 0:512], sgn[0:8, 0:1],
                              rows17[0:8, lh * 512:lh * 512 + 512],
                              start=True, stop=False)
                    te.matmul(pb[2 + lh][0:1, 0:512], onc[0:8, 0:1],
                              b2bf[:, lh * 512:lh * 512 + 512],
                              start=False, stop=False)
                    te.matmul(pb[2 + lh][0:1, 0:512], p01[0:1, 0:1],
                              tbx[lh][0:1, 0:512], start=False, stop=True)
                te.drain().then_inc(mma.h, 1)
                mma.add(1)
                te.wait_ge(aca.h, aca_b + 1)
                for ti in range(NT1):
                    if ti >= 2:
                        te.wait_ge(dvb.h, dvb_b + (ti - 2) + 1)
                    te.transpose(pbf[ti % 2][0:8, 0:128],
                                 wbt[:, ti * 8:ti * 8 + 8], idb[:])
                    te.drain().then_inc(mmc.h, 1)
                    mmc.add(1)
                te.wait_ge(dvb.h, dvb_b + 8)
                for ti in range(NT1):
                    for lh in range(2):
                        i = ti * 2 + lh
                        if i >= 3:
                            te.wait_ge(dvb.h, dvb_b + 8 + i - 2)
                        te.matmul(cb3[i % 3][:, 0:512],
                                  waug[0:8, ti * 128:(ti + 1) * 128],
                                  rows17[0:8, lh * 512:lh * 512 + 512],
                                  start=True, stop=False)
                        te.matmul(cb3[i % 3][:, 0:512], onr[0:1, 0:128],
                                  wslv(0)[0:1, lh * 512:lh * 512 + 512],
                                  start=False, stop=True).then_inc(mmb.h, 1)
                        mmb.add(1)
                        if i == 15:
                            te.drain().then_inc(mmb.h, 1)
                            mmb.add(1)

            @blk.scalar
            def _(sc):
                sc.wait_ge(mma.h, mma_b + 1)
                for lh in range(2):
                    act = sc.activation(wslv(0)[0:1, lh * 512:lh * 512 + 512],
                                        pb[2 + lh][0:1, 0:512], AF.Copy)
                    if lh == 1:
                        act.then_inc(aca.h, 1)
                aca.add(1)

        dump("acc", arC[:, :], (128, 8192), F32)
        dump("rows17", rows17[0:8, :], (8, 1024), BF16)
        dump("cmb", wslv(0)[0:1, 0:1024], (1, 1024), BF16)

        _ck(11)
        with nc.Block("trans") as blk:
            mma_b, dva_b, dvb_b = mma.v, dva.v, dvb.v

            @blk.vector
            def _(v):
                def cast(j):
                    ti, lh = j // 2, j % 2
                    if j >= 2:
                        v.wait_ge(mma.h, mma_b + 4 * (j - 1))
                    v.tensor_copy(tbx[j % 2][:],
                                  accv(ti)[:, lh * 512:lh * 512 + 512]
                                  ).then_inc(dva.h, 1)
                    dva.add(1)

                cast(0)
                cast(1)
                for i in range(16):
                    ti, lh = i // 2, i % 2
                    for q in range(4):
                        k = i * 4 + q
                        if q == 3 and i + 2 < 16:
                            cast(i + 2)
                        v.wait_ge(mma.h, mma_b + min(k + 2, 66))
                        lc = lh * 4 + q
                        v.tensor_copy(moev(lc)[:, ti * 128:(ti + 1) * 128],
                                      pbf[k % 2][:]).then_inc(dvb.h, 1)
                        dvb.add(1)

            @blk.tensor
            def _(te):
                for i in range(16):
                    ti, lh = i // 2, i % 2
                    te.wait_ge(dva.h, dva_b + i + 1)
                    for q in range(4):
                        k = i * 4 + q
                        if k >= 2:
                            te.wait_ge(dvb.h, dvb_b + k - 1)
                        te.transpose(pbf[k % 2][:],
                                     tbx[i % 2][:, q * 128:q * 128 + 128],
                                     idb[:]).then_inc(mma.h, 1)
                        mma.add(1)
                if True:
                    te.drain().then_inc(mma.h, 2)
                    mma.add(2)

        dump("moeT", bfv(arF)[:, 0:8192], (128, 8192), BF16)

        _ck(12)
        with nc.Block("upcore") as blk:
            UC = dict(aca=aca.v, acb=acb.v, mma=mma.v, dva=dva.v,
                      ldp=[s.v for s in ldp], ldq=[s.v for s in ldq])

            @blk.sync
            def _(sy):
                npiece = 0
                nup = 0
                for osl in range(8):
                    for jc in range(NJ):
                        if npiece >= 4:
                            sy.wait_ge(aca.h, UC["aca"] + (npiece - 3))
                        sem = ldp[npiece % 4]
                        sy.dma_start(stgv(npiece),
                                     core_w.ap()[jc * 128:(jc + 1) * 128,
                                                 osl * 256:osl * 256 + 256]
                                     ).then_inc(sem.h, 16)
                        sem.add(16)
                        npiece += 1
                    for lc in range(NL):
                        if nup >= 2:
                            sy.wait_ge(acb.h, UC["acb"] + nup - 1)
                        sem = ldq[nup % 2]
                        sy.dma_start(w2st[nup % 2][:, 0:256],
                                     up_w.ap()[lc * 128:(lc + 1) * 128,
                                               osl * 256:osl * 256 + 256]
                                     ).then_inc(sem.h, 16)
                        sem.add(16)
                        nup += 1

            @blk.scalar
            def _(sc):
                for jc in range(NJ):
                    sc.activation(gx2v(jc)[:, :], y1v(jc), AF.Gelu)
                npiece = 0
                nup = 0
                for osl in range(8):
                    if osl >= 2:
                        sc.wait_ge(mma.h, UC["mma"] + 4 * (osl - 1))
                    for jc in range(NJ):
                        sc.wait_ge(ldp[npiece % 4].h,
                                   UC["ldp"][npiece % 4] + 16 * (npiece // 4 + 1))
                        sc.activation(wslv(osl % 2)[:, jc * 256:(jc + 1) * 256],
                                      stgv(npiece), AF.Copy).then_inc(aca.h, 1)
                        aca.add(1)
                        npiece += 1
                    for lc in range(NL):
                        sc.wait_ge(ldq[nup % 2].h,
                                   UC["ldq"][nup % 2] + 16 * (nup // 2 + 1))
                        sc.activation(upslv(osl % 2)[:, lc * 256:(lc + 1) * 256],
                                      w2st[nup % 2][:, 0:256], AF.Copy
                                      ).then_inc(acb.h, 1)
                        acb.add(1)
                        nup += 1

            @blk.vector
            def _(v):
                chain(v, v.tensor_add(bsum[:, :], ubs[:, :], cbs[:, :]), dvs)
                v.nop().then_inc(dva.h, 1)
                dva.add(1)
                for i in range(32):
                    oc, th = i // 2, i % 2
                    v.wait_ge(mma.h, UC["mma"] + min(i + 2, 33))
                    sl = y1v(oc)[:, th * 512:th * 512 + 512]
                    v.scalar_tensor_tensor(sl, pb[i % 4][:],
                                           bsum[:, oc:oc + 1], sl, ALU.add,
                                           ALU.add).then_inc(dva.h, 1)
                    dva.add(1)

            @blk.tensor
            def _(te):
                i = 0
                for osl in range(8):
                    te.wait_ge(aca.h, UC["aca"] + NJ * (osl + 1))
                    te.wait_ge(acb.h, UC["acb"] + NL * (osl + 1))
                    for k2 in range(2):
                        for th in range(2):
                            if i >= 4:
                                te.wait_ge(dva.h, UC["dva"] + (i - 4) + 2)
                            for jc in range(NJ):
                                te.matmul(
                                    pb[i % 4][:, 0:512],
                                    wslv(osl % 2)[:, jc * 256 + k2 * 128:
                                                  jc * 256 + k2 * 128 + 128],
                                    gx2v(jc)[:, th * 512:th * 512 + 512],
                                    start=(jc == 0), stop=False)
                            for lc in range(NL):
                                mm = te.matmul(
                                    pb[i % 4][:, 0:512],
                                    upslv(osl % 2)[:, lc * 256 + k2 * 128:
                                                   lc * 256 + k2 * 128 + 128],
                                    moev(lc)[:, th * 512:th * 512 + 512],
                                    start=False, stop=(lc == NL - 1))
                                if lc == NL - 1:
                                    mm.then_inc(mma.h, 1)
                            mma.add(1)
                            i += 1
                    if osl == 7:
                        te.drain().then_inc(mma.h, 1)
                        mma.add(1)

        _ck(13)
        dump("y2", arA[:, :], (128, 16384), F32)

        ln_phase("ln2", also_bf16_copy=False, final_store=True,
                 ready_off=lambda th, oc: -31 + 2 * oc + th)


    except _StopBuild:
        pass

    stx.close()
    nc.finalize()
    return nc


def _build_guarded():
    return build_nc()


_cost = np.float32(np.array([2 * 1024 * h for h in HE], dtype=np.float32)
                   * np.float32(1e-7))


def _stage_weights(inp):
    f = np.ascontiguousarray
    m = {}
    m["wqt"] = f(inp["L0_attn_Wq"].T)
    m["wkt"] = f(inp["L0_attn_Wk"].T)
    m["wvt"] = f(inp["L0_attn_Wv"].T)
    m["wot"] = f(inp["L0_attn_Wo"].T)
    m["bq_t"] = f(inp["L0_attn_bq"].reshape(NJ, 128).T)
    m["bk_row"] = f(inp["L0_attn_bk"].reshape(1, 512))
    m["bv_row"] = f(inp["L0_attn_bv"].reshape(1, 512))
    m["down_w"] = f(inp["L0_ffn_down_w"])
    m["down_b_t"] = f(inp["L0_ffn_down_b"].reshape(NL, 128).T)
    m["up_w"] = f(inp["L0_ffn_up_w"])
    m["up_b_t"] = f(inp["L0_ffn_up_b"].reshape(NJ, 128).T)
    m["router_w"] = f(inp["L0_ffn_router_w"])
    m["rb_col"] = f(inp["L0_ffn_router_b"].reshape(8, 1))
    m["ncost_col"] = f((-_cost).reshape(8, 1)).astype(np.float32)
    m["shared_w1"] = f(inp["L0_ffn_shared_w1"])
    m["shared_b1_t"] = f(inp["L0_ffn_shared_b1"].reshape(8, 128).T)
    m["shared_w2"] = f(inp["L0_ffn_shared_w2"])
    m["shared_b2_row"] = f(inp["L0_ffn_shared_b2"].reshape(1, L))
    m["core_w"] = f(inp["L0_ffn_core_w"])
    m["core_b_t"] = f(inp["L0_ffn_core_b"].reshape(NJ, 128).T)
    for e in range(8):
        m[f"w1_{e}"] = f(inp[f"L0_ffn_e{e}_w1"])
        m[f"w2_{e}"] = f(inp[f"L0_ffn_e{e}_w2"])
        m[f"b1_{e}_t"] = f(inp[f"L0_ffn_e{e}_b1"].reshape(NCH[e], 128).T)
    m["b2all"] = f(np.stack([inp[f"L0_ffn_e{e}_b2"] for e in range(8)]))
    m["ident_bf"] = np.eye(128, dtype=ml_dtypes.bfloat16)
    m["ident8_f"] = np.eye(8, dtype=np.float32)
    m["ones_row_bf"] = np.ones((1, 128), dtype=ml_dtypes.bfloat16)
    m["ones_col_bf"] = np.ones((128, 1), dtype=ml_dtypes.bfloat16)
    sg = np.concatenate([-np.ones(8), [0.1], np.ones(8)]).astype(
        ml_dtypes.bfloat16).reshape(17, 1)
    m["sgn17_col"] = sg
    m["p01_col"] = np.full((1, 1), 0.1, dtype=ml_dtypes.bfloat16)
    return m


def kernel(**inputs):
    x = np.asarray(inputs["L0_attn_hidden_states"], dtype=np.float32)
    wmap = _stage_weights({k: np.asarray(v, dtype=np.float32)
                           for k, v in inputs.items()
                           if k != "L0_attn_hidden_states"})
    nc = build_nc()
    in_maps = []
    for c in range(8):
        b, half = c // 2, c % 2
        xtb_ = np.ascontiguousarray(x[b].T)
        own = xtb_[:, half * 1024:(half + 1) * 1024]
        oth = xtb_[:, (1 - half) * 1024:(2 - half) * 1024]
        im = dict(wmap)
        im["xt"] = np.ascontiguousarray(np.concatenate([own, oth], axis=1))
        in_maps.append(im)
    res = run_bass_kernel_spmd(nc, in_maps, core_ids=list(range(8)))
    y = np.empty((4, 2048, 2048), dtype=np.float32)
    for c in range(8):
        b, half = c // 2, c % 2
        y[b, half * 1024:(half + 1) * 1024, :] = res.results[c]["out"].T
    return y
